# version 03
# baseline (speedup 1.0000x reference)
import contextlib
import numpy as np

import concourse.bass as bass
import concourse.mybir as mybir
from concourse.bass_utils import run_bass_kernel_spmd

F32 = mybir.dt.float32
F16 = mybir.dt.float16
BF16 = mybir.dt.bfloat16
AF = mybir.ActivationFunctionType
ALU = mybir.AluOpType
AX = mybir.AxisListType

B, L, DM = 4, 8192, 1024
DI, NS = 512, 256
CHA = 256
INB, DSB = 2048, 64
CHB = 1024
TAPS = 64
EPS = 1e-5

TOK = 512
TS = 1024
NG = L // TS
KPG = TS // TOK

TOK2 = 256
NCH2 = L // TOK2

ENGS = ("sync", "vector", "scalar", "tensor", "gpsimd")


class Sched:

    def __init__(self, sems):
        self.sems = sems
        self.ops = {e: [] for e in ENGS}
        self.cnt = {e: 0 for e in ENGS}
        self.ld = 0
        self.st = 0
        self.last = None

    def _wait_last(self, eng):
        if self.last is None:
            return
        nm, v = self.last
        self.ops[eng].append(lambda e, nm=nm, v=v: e.wait_ge(self.sems[nm], v))

    def _emit(self, eng, fns, chain):
        for fn in fns:
            if chain and self.cnt[eng] > 0:
                self.ops[eng].append(
                    lambda e, nm=eng, v=self.cnt[eng]: e.wait_ge(self.sems[nm], v))
            self.cnt[eng] += 1
            def run(e, fn=fn, nm=eng):
                ins = fn(e)
                ins.then_inc(self.sems[nm], 1)
            self.ops[eng].append(run)
        self.last = (eng, self.cnt[eng])
        return self.last

    def stage(self, eng, *fns):
        assert fns
        self._wait_last(eng)
        return self._emit(eng, fns, True)

    def stage_after(self, eng, events, *fns, chain=True):
        assert fns
        for nm, v in events:
            if v > 0:
                self.ops[eng].append(
                    lambda e, nm=nm, v=v: e.wait_ge(self.sems[nm], v))
        return self._emit(eng, fns, chain)

    def dma(self, fn, store=False, events=()):
        self._wait_last("sync")
        for nm, v in events:
            if v > 0:
                self.ops["sync"].append(
                    lambda e, nm=nm, v=v: e.wait_ge(self.sems[nm], v))
        if store:
            self.st += 16
            nm, v = "st", self.st
        else:
            self.ld += 16
            nm, v = "ld", self.ld
        def run(e, fn=fn, nm=nm):
            fn(e).then_inc(self.sems[nm], 16)
        self.ops["sync"].append(run)
        self.last = (nm, v)

    def finish(self):
        self._wait_last("sync")
        if self.st:
            self.ops["sync"].append(lambda e: e.wait_ge(self.sems["st"], self.st))


def run_streams(nc, sc):
    with nc.Block() as block:
        @block.sync
        def _(e):
            for f in sc.ops["sync"]:
                f(e)

        @block.vector
        def _(e):
            for f in sc.ops["vector"]:
                f(e)

        @block.scalar
        def _(e):
            for f in sc.ops["scalar"]:
                f(e)

        @block.tensor
        def _(e):
            for f in sc.ops["tensor"]:
                f(e)

        @block.gpsimd
        def _(e):
            for f in sc.ops["gpsimd"]:
                f(e)


def flat(ap):
    return ap.rearrange("p a b -> p (a b)")


def ln_full(sc, x, zout, sq, m_t, q_t, rs_t, nm_t, nt, dm=DM):
    sc.stage("scalar", lambda e: e.activation(out=sq[:], in_=x[:], func=AF.Square))
    sc.stage(
        "vector",
        lambda e: e.tensor_reduce(out=q_t[:], in_=sq[:], axis=AX.X, op=ALU.add),
        lambda e: e.tensor_reduce(out=m_t[:], in_=x[:], axis=AX.X, op=ALU.add),
        lambda e: e.tensor_scalar_mul(m_t[:], m_t[:], 1.0 / dm),
        lambda e: e.tensor_scalar_mul(q_t[:], q_t[:], 1.0 / dm),
        lambda e: e.tensor_mul(rs_t[:], m_t[:], m_t[:]),
        lambda e: e.tensor_sub(q_t[:], q_t[:], rs_t[:]),
    )
    sc.stage(
        "scalar",
        lambda e: e.activation(out=rs_t[:], in_=q_t[:], func=AF.Ln, bias=EPS),
        lambda e: e.activation(out=rs_t[:], in_=rs_t[:], func=AF.Exp, scale=-0.5),
    )
    fns = [lambda e: e.tensor_mul(nm_t[:], m_t[:], rs_t[:])]
    for tb in range(nt):
        fns.append(lambda e, tb=tb: e.tensor_scalar(
            out=zout[:, tb, :], in0=x[:, tb, :],
            scalar1=rs_t[:, tb:tb + 1], scalar2=nm_t[:, tb:tb + 1],
            op0=ALU.mult, op1=ALU.subtract))
    sc.stage("vector", *fns)


def sigmoid_chain(sc, src, dst, scr, n):
    sc.stage("scalar", lambda e: e.activation(out=scr, in_=src, func=AF.Exp, scale=-1.0))
    sc.stage(
        "vector",
        lambda e: e.tensor_scalar_add(scr, scr, 1.0),
        lambda e: e.reciprocal(out=scr, in_=scr),
        lambda e: e.tensor_copy(dst, scr),
    )



def setup_consts(nc, sc, consts):
    sc.stage(
        "vector",
        lambda e: e.memset(consts[:, 0:1], EPS),
        lambda e: e.memset(consts[:, 1:2], 1.0),
    )
    nc.const_aps.register(F32, EPS, consts[:, 0:1])
    nc.const_aps.register(F32, 1.0, consts[:, 1:2])

def build_l1():
    nc = bass.Bass("TRN2", target_bir_lowering=False)

    din = {}
    def I(name, shape):
        din[name] = nc.dram_tensor(name, shape, F32, kind="ExternalInput").ap()
        return din[name]

    dr_x0 = I("x0", [L, DM])
    I("gam_rep", [128, DM]); I("bet_rep", [128, DM]); I("bdcol", [128, 8])
    I("wup_l", [128, 8, DI]); I("wup_r", [128, 8, CHA])
    I("bup_l", [128, 4]); I("bup_r", [128, 2])
    I("convw", [128, 4, 3]); I("convb", [128, 4])
    I("wdelta", [128, 4, CHA]); I("bdelta", [128, 2])
    I("wb", [128, 4, NS]); I("wc", [128, 4, NS])
    I("alog", [128, 2, NS]); I("dvec", [128, 2])
    I("wdown", [128, 2, DM]); I("ident", [128, 128])

    dr_parta = nc.dram_tensor("parta", [L, DM], F32, kind="ExternalOutput").ap()
    dr_partb = nc.dram_tensor("partb", [L, DM], F32, kind="ExternalOutput").ap()

    x0_t = dr_x0.rearrange("(k tb p) d -> k p tb d", p=128, tb=TOK // 128)
    pa_t = dr_parta.rearrange("(k tb p) d -> k p tb d", p=128, tb=TOK // 128)
    pb_t = dr_partb.rearrange("(k tb p) d -> k p tb d", p=128, tb=TOK // 128)

    with contextlib.ExitStack() as ctx:
        sb = lambda nm, shp, dt: ctx.enter_context(nc.sbuf_tensor(nm, shp, dt))
        ps = lambda nm, shp: ctx.enter_context(nc.psum_tensor(nm, shp, F32))
        sems = {nm: ctx.enter_context(nc.semaphore(nm))
                for nm in ("ld", "st", *ENGS)}
        sc = Sched(sems)
        CONSTS = sb("consts", [128, 2], F32)
        setup_consts(nc, sc, CONSTS)

        W_UPL = sb("w_upl", [128, 8, DI], F16)
        W_UPR = sb("w_upr", [128, 8, CHA], F16)
        W_DEL = sb("w_del", [128, 4, CHA], F16)
        W_B = sb("w_b", [128, 4, NS], F16)
        W_C = sb("w_c", [128, 4, NS], F16)
        W_DN = sb("w_dn", [128, 2, DM], F16)
        GAM = sb("gam", [128, DM], F32)
        BET = sb("bet", [128, DM], F32)
        BDCOL = sb("bdcol_s", [128, 8], F32)
        BUPL = sb("bupl", [128, 4], F32)
        BUPR = sb("bupr", [128, 2], F32)
        CONVW = sb("convw_s", [128, 4, 3], F32)
        CONVB = sb("convb_s", [128, 4], F32)
        BDEL = sb("bdel", [128, 2], F32)
        A_SB = sb("a_sb", [128, 2, NS], F32)
        DVEC = sb("dvec_s", [128, 2], F32)
        IDENT = sb("ident_s", [128, 128], F32)
        IDENTB = sb("identb", [128, 128], F16)

        X0 = sb("x0t", [128, 4, DM], F32)
        YN = sb("yn", [128, 4, DM], F32)
        YT = sb("yt", [128, 8, TOK], F16)
        LEFTX = sb("leftx", [128, 4, TOK + 2], F32)
        RIGHT = sb("right", [128, 2, TOK], F32)
        CONVT = sb("convt", [128, 4, TOK], F32)
        DRAW = sb("draw", [128, 2, TOK], F32)
        U_BF = sb("u_bf", [128, 4, TOK], F16)
        M_T = sb("m_t", [128, 4], F32)
        Q_T = sb("q_t", [128, 4], F32)
        RS_T = sb("rs_t", [128, 4], F32)
        NM_T = sb("nm_t", [128, 4], F32)

        DELTA = sb("delta", [128, 2, TS], F32)
        DELU = sb("delu", [128, 2, TS], F16)
        BMT = sb("bmt", [128, 2, TS], F16)
        CMT = sb("cmt", [128, 2, TS], F16)
        SILR = sb("silr", [128, 2, TS], F16)
        YACC = sb("yacc", [128, 2, TS], F32)
        UDT = sb("udt", [128, 2, TS], BF16)
        CARRY = sb("carry", [128, 2, NS], F32)
        IDENT16 = sb("ident16", [128, 128], BF16)
        SQT = sb("sqt", [128, 4, DM], F32)

        DA = [sb("da0", [128, 2, TS], F32), sb("da1", [128, 2, TS], F32)]
        BMBC = [sb("bmbc0", [128, TS], F16), sb("bmbc1", [128, TS], F16)]
        CCBC = [sb("ccbc0", [128, TS], F16), sb("ccbc1", [128, TS], F16)]
        DAC1 = [sb("dac1a", [128, 1], F32), sb("dac1b", [128, 1], F32)]
        D1 = sb("d1", [128, 2, TS], F16)
        SS = [sb("ss0", [128, 2, TS], BF16), sb("ss1", [128, 2, TS], BF16)]

        P8 = ps("p8", [128, 4096])
        YPS = P8[:, 0:2048]
        BCPB = P8[:, 2048:3072]
        BCPC = P8[:, 3072:4096]
        MM0 = P8[:, 2048:2560]
        PT = P8[:, 3584:4096]

        casts = (("wup_l", W_UPL), ("wup_r", W_UPR), ("wdelta", W_DEL),
                 ("wb", W_B), ("wc", W_C), ("wdown", W_DN))
        for nm, dst in casts:
            n = int(np.prod(din[nm].shape[1:]))
            scr = flat(YN)[:, 0:n]
            sc.dma(lambda e, nm=nm, scr=scr: e.dma_start(out=scr, in_=flat(din[nm])))
            sc.stage("vector", lambda e, dst=dst, scr=scr: e.tensor_copy(flat(dst), scr))
        for nm, dst in (("gam_rep", GAM), ("bet_rep", BET), ("bdcol", BDCOL),
                        ("bup_l", BUPL), ("bup_r", BUPR), ("convw", CONVW),
                        ("convb", CONVB), ("bdelta", BDEL), ("alog", A_SB),
                        ("dvec", DVEC), ("ident", IDENT)):
            sc.dma(lambda e, nm=nm, dst=dst: e.dma_start(out=dst[:], in_=din[nm]))
        sc.stage(
            "vector",
            lambda e: e.tensor_copy(IDENTB[:], IDENT[:]),
            lambda e: e.tensor_copy(IDENT16[:], IDENT[:]),
            lambda e: e.memset(CARRY[:], 0.0),
            lambda e: e.memset(LEFTX[:, :, 0:2], 0.0),
            lambda e: e.memset(DA[0][:, 1, 0:1], 0.0),
            lambda e: e.memset(DA[1][:, 1, 0:1], 0.0),
        )
        sc.stage("scalar", lambda e: e.activation(out=A_SB[:], in_=A_SB[:], func=AF.Exp))
        sc.stage("vector", lambda e: e.tensor_scalar_mul(A_SB[:], A_SB[:], -1.0))

        pe_ev, act_ev, dve_ev = {}, {}, {}
        for g in range(NG):
            for kk in range(KPG):
                k = g * KPG + kk
                ksl = slice(kk * TOK, (kk + 1) * TOK)

                sc.dma(lambda e, k=k: e.dma_start(out=X0[:], in_=x0_t[k]),
                       events=[("st", sc.st)])

                ln_full(sc, X0, YN, SQT, M_T, Q_T, RS_T, NM_T, 4)
                sc.stage("vector", lambda e: e.tensor_add(X0[:], X0[:], YN[:]))
                sc.dma(lambda e, k=k: e.dma_start(out=pa_t[k], in_=X0[:]), store=True)
                ln_full(sc, YN, YN, SQT, M_T, Q_T, RS_T, NM_T, 4)
                gm = GAM[:].unsqueeze(1).broadcast_to([128, 4, DM])
                bt = BET[:].unsqueeze(1).broadcast_to([128, 4, DM])
                sc.stage(
                    "vector",
                    lambda e, gm=gm: e.tensor_tensor(out=YN[:], in0=YN[:], in1=gm,
                                                     op=ALU.mult),
                    lambda e, bt=bt: e.tensor_tensor(out=YN[:], in0=YN[:], in1=bt,
                                                     op=ALU.add),
                )

                for db in range(8):
                    sc.stage("tensor", *[
                        (lambda e, db=db, tb=tb: e.transpose(
                            PT[:, tb * 128:(tb + 1) * 128],
                            YN[:, tb, db * 128:(db + 1) * 128], IDENT[:]))
                        for tb in range(4)])
                    sc.stage("scalar", lambda e, db=db: e.activation(
                        out=YT[:, db, :], in_=PT[:], func=AF.Copy))

                for i, (wt, nob) in enumerate(((W_UPL, 4), (W_UPR, 2))):
                    for ob in range(nob):
                        sc.stage("tensor", *[
                            (lambda e, wt=wt, kb=kb, ob=ob: e.matmul(
                                MM0[:], wt[:, kb, ob * 128:(ob + 1) * 128],
                                YT[:, kb, :], start=(kb == 0), stop=(kb == 7)))
                            for kb in range(8)])
                        if i == 0:
                            sc.stage("vector", lambda e, ob=ob: e.tensor_scalar(
                                out=LEFTX[:, ob, 2:2 + TOK], in0=MM0[:],
                                scalar1=BUPL[:, ob:ob + 1], scalar2=None,
                                op0=ALU.add))
                        else:
                            sc.stage("vector", lambda e, ob=ob: e.tensor_scalar(
                                out=RIGHT[:, ob, :], in0=MM0[:],
                                scalar1=BUPR[:, ob:ob + 1], scalar2=None,
                                op0=ALU.add))

                convfns = []
                for ob in range(4):
                    convfns.append(lambda e, ob=ob: e.tensor_scalar(
                        out=CONVT[:, ob, :], in0=LEFTX[:, ob, 2:2 + TOK],
                        scalar1=CONVW[:, ob, 2:3], scalar2=None, op0=ALU.mult))
                    convfns.append(lambda e, ob=ob: e.scalar_tensor_tensor(
                        out=CONVT[:, ob, :], in0=LEFTX[:, ob, 1:1 + TOK],
                        scalar=CONVW[:, ob, 1:2], in1=CONVT[:, ob, :],
                        op0=ALU.mult, op1=ALU.add))
                    convfns.append(lambda e, ob=ob: e.scalar_tensor_tensor(
                        out=CONVT[:, ob, :], in0=LEFTX[:, ob, 0:TOK],
                        scalar=CONVW[:, ob, 0:1], in1=CONVT[:, ob, :],
                        op0=ALU.mult, op1=ALU.add))
                convfns.append(lambda e: e.tensor_copy(LEFTX[:, :, 0:2],
                                                       LEFTX[:, :, TOK:TOK + 2]))
                sc.stage("vector", *convfns)
                sc.stage("vector", *[
                    (lambda e, ob=ob: e.tensor_scalar(
                        out=CONVT[:, ob, :], in0=CONVT[:, ob, :],
                        scalar1=CONVB[:, ob:ob + 1], scalar2=None, op0=ALU.add))
                    for ob in range(4)])
                sc.stage("scalar", lambda e: e.activation(
                    out=flat(YN)[:, 0:4 * TOK], in_=flat(CONVT), func=AF.Exp,
                    scale=-1.0))
                ynf = flat(YN)
                sc.stage(
                    "vector",
                    lambda e: e.tensor_scalar_add(ynf[:, 0:4 * TOK],
                                                  ynf[:, 0:4 * TOK], 1.0),
                    lambda e: e.reciprocal(out=ynf[:, 0:4 * TOK],
                                           in_=ynf[:, 0:4 * TOK]),
                    lambda e: e.tensor_tensor(out=flat(CONVT), in0=flat(CONVT),
                                              in1=ynf[:, 0:4 * TOK], op=ALU.mult),
                    lambda e: e.tensor_copy(U_BF[:], CONVT[:]),
                    lambda e, ksl=ksl: e.tensor_scalar(
                        out=UDT[:, 0, ksl], in0=CONVT[:, 0, :],
                        scalar1=DVEC[:, 0:1], scalar2=None, op0=ALU.mult),
                    lambda e, ksl=ksl: e.tensor_scalar(
                        out=UDT[:, 1, ksl], in0=CONVT[:, 1, :],
                        scalar1=DVEC[:, 1:2], scalar2=None, op0=ALU.mult),
                )

                sc.stage("scalar", lambda e: e.activation(
                    out=ynf[:, 0:2 * TOK], in_=flat(RIGHT), func=AF.Exp, scale=-1.0))
                srl = SILR[:, :, ksl]
                yn3 = ynf[:, 0:2 * TOK].rearrange("p (a b) -> p a b", a=2)
                sc.stage(
                    "vector",
                    lambda e: e.tensor_scalar_add(ynf[:, 0:2 * TOK],
                                                  ynf[:, 0:2 * TOK], 1.0),
                    lambda e: e.reciprocal(out=ynf[:, 0:2 * TOK],
                                           in_=ynf[:, 0:2 * TOK]),
                    lambda e, srl=srl, yn3=yn3: e.tensor_tensor(
                        out=srl, in0=RIGHT[:], in1=yn3, op=ALU.mult),
                )

                for pb in range(2):
                    sc.stage("tensor", *[
                        (lambda e, pb=pb, kb=kb: e.matmul(
                            MM0[:], W_DEL[:, kb, pb * 128:(pb + 1) * 128],
                            U_BF[:, kb, :], start=(kb == 0), stop=(kb == 3)))
                        for kb in range(4)])
                    sc.stage("vector", lambda e, pb=pb: e.tensor_scalar(
                        out=DRAW[:, pb, :], in0=MM0[:],
                        scalar1=BDEL[:, pb:pb + 1], scalar2=None, op0=ALU.add))
                for wt, dstt in ((W_B, BMT), (W_C, CMT)):
                    for pb in range(2):
                        sc.stage("tensor", *[
                            (lambda e, wt=wt, pb=pb, kb=kb: e.matmul(
                                MM0[:], wt[:, kb, pb * 128:(pb + 1) * 128],
                                U_BF[:, kb, :], start=(kb == 0), stop=(kb == 3)))
                            for kb in range(4)])
                        sc.stage("scalar", lambda e, dstt=dstt, pb=pb, ksl=ksl:
                                 e.activation(out=dstt[:, pb, ksl], in_=MM0[:],
                                              func=AF.Copy))

                drf = DRAW[:]
                dsl = DELTA[:, :, ksl]
                yn3 = ynf[:, 0:2 * TOK].rearrange("p (a b) -> p a b", a=2)
                sc.stage(
                    "scalar",
                    lambda e, dsl=dsl: e.activation(out=dsl, in_=drf, func=AF.Relu),
                    lambda e, yn3=yn3: e.activation(out=yn3, in_=drf, func=AF.Abs),
                    lambda e: e.activation(out=ynf[:, 0:2 * TOK],
                                           in_=ynf[:, 0:2 * TOK], func=AF.Exp,
                                           scale=-1.0),
                    lambda e: e.activation(out=ynf[:, 0:2 * TOK],
                                           in_=ynf[:, 0:2 * TOK], func=AF.Ln,
                                           bias=1.0),
                )
                dusl = DELU[:, :, ksl]
                cu2 = CONVT[:, 0:2, :]
                sc.stage(
                    "vector",
                    lambda e, dsl=dsl, yn3=yn3: e.tensor_tensor(
                        out=dsl, in0=dsl, in1=yn3, op=ALU.add),
                    lambda e, dusl=dusl, dsl=dsl, cu2=cu2: e.tensor_tensor(
                        out=dusl, in0=dsl, in1=cu2, op=ALU.mult),
                )

            tok_end = sc.last
            for n in range(NS):
                it = g * NS + n
                buf = it % 2
                nb, nn = divmod(n, 128)
                sel = IDENTB[:, nn:nn + 1].broadcast_to([128, 128])

                pe_fns = []
                pe_evs = []
                if n == 0:
                    pe_evs.append(tok_end)
                    udf = flat(UDT)
                    for s4 in range(4):
                        pe_fns.append(lambda e, s4=s4: e.matmul(
                            YPS[:, s4 * 512:(s4 + 1) * 512], IDENT16[:],
                            udf[:, s4 * 512:(s4 + 1) * 512],
                            start=True, stop=True))
                if it >= 1:
                    pe_evs.append(act_ev[it - 1])
                for s2 in range(2):
                    pe_fns.append(lambda e, sel=sel, nb=nb, s2=s2: e.matmul(
                        BCPB[:, s2 * 512:(s2 + 1) * 512], sel,
                        BMT[:, nb, s2 * 512:(s2 + 1) * 512],
                        start=True, stop=True))
                    pe_fns.append(lambda e, sel=sel, nb=nb, s2=s2: e.matmul(
                        BCPC[:, s2 * 512:(s2 + 1) * 512], sel,
                        CMT[:, nb, s2 * 512:(s2 + 1) * 512],
                        start=True, stop=True))
                if n >= 1:
                    pe_evs.append(dve_ev[it - 1])
                    ssf_p = flat(SS[(it - 1) % 2])
                    for s4 in range(4):
                        pe_fns.append(lambda e, ssf_p=ssf_p, s4=s4: e.matmul(
                            YPS[:, s4 * 512:(s4 + 1) * 512], IDENT16[:],
                            ssf_p[:, s4 * 512:(s4 + 1) * 512],
                            start=False, stop=True))
                pe_ev[it] = sc.stage_after("tensor", pe_evs, *pe_fns, chain=False)

                act_evs = [pe_ev[it]]
                if it >= 2:
                    act_evs.append(dve_ev[it - 2])
                act_ev[it] = sc.stage_after(
                    "scalar", act_evs,
                    lambda e, buf=buf: e.activation(out=BMBC[buf][:], in_=BCPB[:],
                                                    func=AF.Copy),
                    lambda e, buf=buf: e.activation(out=CCBC[buf][:], in_=BCPC[:],
                                                    func=AF.Copy),
                    lambda e, buf=buf, n=n: e.activation(
                        out=DA[buf][:, 0, :], in_=DELTA[:, 0, :], func=AF.Exp,
                        scale=A_SB[:, 0, n:n + 1]),
                    lambda e, buf=buf, n=n: e.activation(
                        out=DA[buf][:, 1, 1:TS], in_=DELTA[:, 1, 1:TS],
                        func=AF.Exp, scale=A_SB[:, 1, n:n + 1]),
                    lambda e, buf=buf, n=n: e.activation(
                        out=DAC1[buf][:], in_=DELTA[:, 1, 0:1], func=AF.Exp,
                        scale=A_SB[:, 1, n:n + 1]),
                    chain=False)

                dve_evs = [act_ev[it]]
                if it >= 1:
                    dve_evs.append(pe_ev[it])
                bmb = BMBC[buf][:].unsqueeze(1).broadcast_to([128, 2, TS])
                ccb = CCBC[buf][:].unsqueeze(1).broadcast_to([128, 2, TS])
                ssf = flat(SS[buf])
                dve_ev[it] = sc.stage_after(
                    "vector", dve_evs,
                    lambda e, bmb=bmb: e.tensor_tensor(
                        out=D1[:], in0=DELU[:], in1=bmb, op=ALU.mult),
                    lambda e, buf=buf, n=n: e.scalar_tensor_tensor(
                        out=D1[:, 1, 0:1], in0=DAC1[buf][:],
                        scalar=CARRY[:, 1, n:n + 1], in1=D1[:, 1, 0:1],
                        op0=ALU.mult, op1=ALU.add),
                    lambda e, buf=buf, n=n: e.tensor_tensor_scan(
                        out=flat(SS[buf]), data0=flat(DA[buf]), data1=flat(D1),
                        initial=CARRY[:, 0, n:n + 1], op0=ALU.mult, op1=ALU.add),
                    lambda e, buf=buf, n=n: e.tensor_copy(
                        CARRY[:, :, n:n + 1], SS[buf][:, :, TS - 1:TS]),
                    lambda e, buf=buf, ccb=ccb: e.tensor_tensor(
                        out=SS[buf][:], in0=SS[buf][:], in1=ccb, op=ALU.mult),
                )

            itl = g * NS + NS - 1
            ssf_l = flat(SS[itl % 2])
            sc.stage_after("tensor", [dve_ev[itl]], *[
                (lambda e, ssf_l=ssf_l, s4=s4: e.matmul(
                    YPS[:, s4 * 512:(s4 + 1) * 512], IDENT16[:],
                    ssf_l[:, s4 * 512:(s4 + 1) * 512], start=False, stop=True))
                for s4 in range(4)], chain=False)
            sc.stage("scalar", lambda e: e.activation(
                out=flat(YACC), in_=YPS, func=AF.Copy))

            sc.stage("vector", lambda e: e.tensor_tensor(
                out=SS[0][:], in0=YACC[:], in1=SILR[:], op=ALU.mult))
            sc.stage_after("vector", [("st", sc.st)],
                           lambda e: e.engine_nop())
            for sub in range(KPG):
                tsl = slice(sub * TOK, (sub + 1) * TOK)
                k = g * KPG + sub
                for ob in range(8):
                    sc.stage("tensor", *[
                        (lambda e, ob=ob, dblk=dblk, tsl=tsl: e.matmul(
                            MM0[:], W_DN[:, dblk, ob * 128:(ob + 1) * 128],
                            SS[0][:, dblk, tsl], start=(dblk == 0),
                            stop=(dblk == 1)))
                        for dblk in range(2)])
                    sc.stage("vector", lambda e, ob=ob: e.tensor_scalar(
                        out=flat(CONVT)[:, 0:TOK], in0=MM0[:],
                        scalar1=BDCOL[:, ob:ob + 1], scalar2=None, op0=ALU.add))
                    sc.stage("tensor", *[
                        (lambda e, tb=tb: e.transpose(
                            PT[:, tb * 128:(tb + 1) * 128],
                            flat(CONVT)[:, tb * 128:(tb + 1) * 128], IDENT[:]))
                        for tb in range(4)])
                    sc.stage("scalar", lambda e, ob=ob: e.activation(
                        out=X0[:, :, ob * 128:(ob + 1) * 128],
                        in_=PT[:].rearrange("p (tb c) -> p tb c", tb=4),
                        func=AF.Copy))
                sc.dma(lambda e, k=k: e.dma_start(out=pb_t[k], in_=X0[:]),
                       store=True)

        sc.finish()
        run_streams(nc, sc)
    return nc


def build_l2():
    nc = bass.Bass("TRN2", target_bir_lowering=False)

    din = {}
    def I(name, shape):
        din[name] = nc.dram_tensor(name, shape, F32, kind="ExternalInput").ap()
        return din[name]

    I("parta", [L, DM]); I("partb0", [L, DM]); I("partb1", [L, DM])
    I("lng_rep", [128, DM]); I("lnb_rep", [128, DM])
    I("gamb_rep", [128, DM]); I("betb_rep", [128, DM])
    I("inw", [128, 8, 3 * CHB]); I("inb", [128, 24])
    I("convw2", [128, 8, 3]); I("convb2", [128, 8])
    I("alogb", [128, 8, DSB]); I("bp", [128, 8, DSB]); I("cp", [128, 8, DSB])
    I("dp", [128, 8])
    I("outw", [128, 8, DM]); I("outb_rep", [128, DM])
    I("ident", [128, 128])

    dr_x2 = nc.dram_tensor("x2", [L, DM], F32, kind="ExternalOutput").ap()
    dr_pa2 = nc.dram_tensor("parta2", [L, DM], F32, kind="ExternalOutput").ap()
    dr_pb2 = nc.dram_tensor("partb2", [L, DM], F32, kind="ExternalOutput").ap()

    NT = TOK2 // 128
    t_in = {nm: din[nm].rearrange("(k tb p) d -> k p tb d", p=128, tb=NT)
            for nm in ("parta", "partb0", "partb1")}
    x2_t = dr_x2.rearrange("(k tb p) d -> k p tb d", p=128, tb=NT)
    pa2_t = dr_pa2.rearrange("(k tb p) d -> k p tb d", p=128, tb=NT)
    pb2_t = dr_pb2.rearrange("(k tb p) d -> k p tb d", p=128, tb=NT)

    with contextlib.ExitStack() as ctx:
        sb = lambda nm, shp, dt: ctx.enter_context(nc.sbuf_tensor(nm, shp, dt))
        ps = lambda nm, shp: ctx.enter_context(nc.psum_tensor(nm, shp, F32))
        sems = {nm: ctx.enter_context(nc.semaphore(nm))
                for nm in ("ld", "st", *ENGS)}
        sc = Sched(sems)
        CONSTS = sb("consts2", [128, 2], F32)
        setup_consts(nc, sc, CONSTS)

        INW = sb("inw_s", [128, 8, 3 * CHB], F16)
        OUTW = sb("outw_s", [128, 8, DM], F16)
        LNG = sb("lng", [128, DM], F32)
        LNB = sb("lnb", [128, DM], F32)
        GAMB = sb("gamb", [128, DM], F32)
        BETB = sb("betb", [128, DM], F32)
        OUTB = sb("outb", [128, DM], F32)
        INBS = sb("inb_s", [128, 24], F32)
        CONVW2 = sb("convw2_s", [128, 8, 3], F32)
        CONVB2 = sb("convb2_s", [128, 8], F32)
        ALOGB = sb("alogb_s", [128, 8, DSB], F32)
        BP = sb("bp_s", [128, 8, DSB], F32)
        CP = sb("cp_s", [128, 8, DSB], F32)
        DP = sb("dp_s", [128, 8], F32)
        KT = sb("kt", [128, 8, TAPS], F32)
        PIT = sb("pit", [128, 8, DSB], F32)
        IDENT = sb("ident2", [128, 128], F32)

        XA = sb("xa", [128, NT, DM], F32)
        XB = sb("xb", [128, NT, DM], F32)
        XC = sb("xc", [128, NT, DM], F32)
        XT = sb("xt", [128, 8, TOK2], F16)
        M_T = sb("m_t2", [128, NT], F32)
        Q_T = sb("q_t2", [128, NT], F32)
        RS_T = sb("rs_t2", [128, NT], F32)
        NM_T = sb("nm_t2", [128, NT], F32)

        URAW = [sb("uraw0", [128, 8, TOK2], F32), sb("uraw1", [128, 8, TOK2], F32)]
        SILV = [sb("silv0", [128, 8, TOK2], F16), sb("silv1", [128, 8, TOK2], F16)]
        SGG = [sb("sgg0", [128, 8, TOK2], F16), sb("sgg1", [128, 8, TOK2], F16)]
        UCX = sb("ucx", [128, 8, TAPS + TOK2], F16)
        UTAIL = sb("utail", [128, 8, 1], F32)
        CVT = sb("cvt", [128, 8, TOK2], F32)
        YB = sb("yb", [128, 8, TOK2], F32)
        GDB = sb("gdb", [128, 8, TOK2], F16)
        SCR = sb("scr", [128, 8 * TOK2], F32)

        PT = ps("pt2", [128, 512])
        MM0 = ps("mm02", [128, 512])

        for nm, dst in (("inw", INW), ("outw", OUTW)):
            total = int(np.prod(din[nm].shape[1:]))
            CHK = NT * DM
            src_f = flat(din[nm])
            dst_f = flat(dst)
            for off in range(0, total, CHK):
                w = min(CHK, total - off)
                scr = flat(XA)[:, 0:w]
                src = src_f[:, off:off + w]
                dst_h = dst_f[:, off:off + w]
                sc.dma(lambda e, src=src, scr=scr: e.dma_start(out=scr, in_=src))
                sc.stage("vector", lambda e, dst_h=dst_h, scr=scr:
                         e.tensor_copy(dst_h, scr))
        for nm, dst in (("lng_rep", LNG), ("lnb_rep", LNB), ("gamb_rep", GAMB),
                        ("betb_rep", BETB), ("outb_rep", OUTB), ("inb", INBS),
                        ("convw2", CONVW2), ("convb2", CONVB2), ("alogb", ALOGB),
                        ("bp", BP), ("cp", CP), ("dp", DP), ("ident", IDENT)):
            sc.dma(lambda e, nm=nm, dst=dst: e.dma_start(out=dst[:], in_=din[nm]))

        sc.stage("scalar",
                 lambda e: e.activation(out=ALOGB[:], in_=ALOGB[:], func=AF.Exp))
        sc.stage("scalar",
                 lambda e: e.activation(out=ALOGB[:], in_=ALOGB[:], func=AF.Exp,
                                        scale=-1.0))
        kfns = [lambda e: e.tensor_mul(PIT[:], BP[:], CP[:]),
                lambda e: e.tensor_reduce(out=KT[:, :, 0:1], in_=PIT[:],
                                          axis=AX.X, op=ALU.add)]
        for d in range(1, TAPS):
            kfns.append(lambda e: e.tensor_mul(PIT[:], PIT[:], ALOGB[:]))
            kfns.append(lambda e, d=d: e.tensor_reduce(
                out=KT[:, :, d:d + 1], in_=PIT[:], axis=AX.X, op=ALU.add))
        kfns.append(lambda e: e.memset(UCX[:, :, 0:TAPS], 0.0))
        sc.stage("vector", *kfns)

        def attnb_tail(kprev, cur_first):
            p = kprev % 2
            q = (kprev + 1) % 2
            fns = []
            for cb in range(8):
                fns.append(lambda e, cb=cb, p=p: e.tensor_scalar(
                    out=CVT[:, cb, :], in0=URAW[p][:, cb, :],
                    scalar1=CONVW2[:, cb, 1:2], scalar2=None, op0=ALU.mult))
                fns.append(lambda e, cb=cb, p=p, q=q: e.scalar_tensor_tensor(
                    out=CVT[:, cb, 1:TOK2], in0=URAW[p][:, cb, 0:TOK2 - 1],
                    scalar=CONVW2[:, cb, 0:1], in1=CVT[:, cb, 1:TOK2],
                    op0=ALU.mult, op1=ALU.add))
                if kprev > 0:
                    fns.append(lambda e, cb=cb: e.scalar_tensor_tensor(
                        out=CVT[:, cb, 0:1], in0=UTAIL[:, cb, :],
                        scalar=CONVW2[:, cb, 0:1], in1=CVT[:, cb, 0:1],
                        op0=ALU.mult, op1=ALU.add))
                fns.append(lambda e, cb=cb, p=p: e.scalar_tensor_tensor(
                    out=CVT[:, cb, 0:TOK2 - 1], in0=URAW[p][:, cb, 1:TOK2],
                    scalar=CONVW2[:, cb, 2:3], in1=CVT[:, cb, 0:TOK2 - 1],
                    op0=ALU.mult, op1=ALU.add))
                if cur_first is not None:
                    fns.append(lambda e, cb=cb, cur_first=cur_first:
                               e.scalar_tensor_tensor(
                                   out=CVT[:, cb, TOK2 - 1:TOK2],
                                   in0=cur_first[:, cb, 0:1],
                                   scalar=CONVW2[:, cb, 2:3],
                                   in1=CVT[:, cb, TOK2 - 1:TOK2],
                                   op0=ALU.mult, op1=ALU.add))
            sc.stage("vector", *fns)
            sc.stage("vector", *[
                (lambda e, cb=cb: e.tensor_scalar(
                    out=CVT[:, cb, :], in0=CVT[:, cb, :],
                    scalar1=CONVB2[:, cb:cb + 1], scalar2=None, op0=ALU.add))
                for cb in range(8)])
            sc.stage("scalar", lambda e: e.activation(
                out=SCR[:], in_=flat(CVT), func=AF.Exp, scale=-1.0))
            sc.stage(
                "vector",
                lambda e: e.tensor_scalar_add(SCR[:], SCR[:], 1.0),
                lambda e: e.reciprocal(out=SCR[:], in_=SCR[:]),
                lambda e: e.tensor_tensor(out=flat(CVT), in0=flat(CVT),
                                          in1=SCR[:], op=ALU.mult),
                lambda e: e.tensor_copy(UCX[:, :, TAPS:], CVT[:]),
            )
            tapfns = []
            for cb in range(8):
                tapfns.append(lambda e, cb=cb: e.tensor_scalar(
                    out=YB[:, cb, :], in0=CVT[:, cb, :],
                    scalar1=DP[:, cb:cb + 1], scalar2=None, op0=ALU.mult))
            for d in range(TAPS):
                for cb in range(8):
                    tapfns.append(lambda e, cb=cb, d=d: e.scalar_tensor_tensor(
                        out=YB[:, cb, :], in0=UCX[:, cb, TAPS - d:TAPS - d + TOK2],
                        scalar=KT[:, cb, d:d + 1], in1=YB[:, cb, :],
                        op0=ALU.mult, op1=ALU.add))
            tapfns.append(lambda e: e.tensor_copy(UCX[:, :, 0:TAPS],
                                                  UCX[:, :, TOK2:]))
            tapfns.append(lambda e, p=p: e.tensor_tensor(
                out=flat(CVT), in0=flat(YB), in1=flat(SGG[p]), op=ALU.mult))
            tapfns.append(lambda e, p=p: e.tensor_tensor(
                out=flat(GDB), in0=flat(CVT), in1=flat(SILV[p]), op=ALU.add))
            sc.stage("vector", *tapfns)
            for ob in range(8):
                sc.stage("tensor", *[
                    (lambda e, ob=ob, cb=cb: e.matmul(
                        MM0[:, 0:TOK2], OUTW[:, cb, ob * 128:(ob + 1) * 128],
                        GDB[:, cb, :], start=(cb == 0), stop=(cb == 7)))
                    for cb in range(8)])
                sc.stage("scalar", lambda e: e.activation(
                    out=flat(CVT)[:, 0:TOK2], in_=MM0[:, 0:TOK2], func=AF.Copy))
                sc.stage("tensor", *[
                    (lambda e, tb=tb: e.transpose(
                        PT[:, tb * 128:(tb + 1) * 128],
                        flat(CVT)[:, tb * 128:(tb + 1) * 128], IDENT[:]))
                    for tb in range(NT)])
                sc.stage("scalar", lambda e, ob=ob: e.activation(
                    out=XC[:, :, ob * 128:(ob + 1) * 128],
                    in_=PT[:, 0:NT * 128].rearrange("p (tb c) -> p tb c", tb=NT),
                    func=AF.Copy))
            sc.dma(lambda e, kprev=kprev: e.dma_start(out=pb2_t[kprev], in_=XC[:]),
                   store=True)

        for k in range(NCH2):
            par = k % 2
            sc.dma(lambda e, k=k: e.dma_start(out=XA[:], in_=t_in["parta"][k]))
            sc.dma(lambda e, k=k: e.dma_start(out=XB[:], in_=t_in["partb0"][k]))
            sc.stage("vector", lambda e: e.tensor_add(XA[:], XA[:], XB[:]))
            sc.dma(lambda e, k=k: e.dma_start(out=XB[:], in_=t_in["partb1"][k]))
            sc.stage("vector", lambda e: e.tensor_add(XA[:], XA[:], XB[:]))
            ln_full(sc, XA, XB, XC, M_T, Q_T, RS_T, NM_T, NT)
            ln_full(sc, XB, XC, XC, M_T, Q_T, RS_T, NM_T, NT)
            lg = LNG[:].unsqueeze(1).broadcast_to([128, NT, DM])
            lb = LNB[:].unsqueeze(1).broadcast_to([128, NT, DM])
            sc.stage(
                "vector",
                lambda e, lg=lg: e.tensor_tensor(out=XC[:], in0=XC[:], in1=lg,
                                                 op=ALU.mult),
                lambda e, lb=lb: e.tensor_tensor(out=XC[:], in0=XC[:], in1=lb,
                                                 op=ALU.add),
                lambda e: e.tensor_add(XA[:], XA[:], XC[:]),
            )
            sc.dma(lambda e, k=k: e.dma_start(out=x2_t[k], in_=XA[:]), store=True)
            ln_full(sc, XA, XB, XC, M_T, Q_T, RS_T, NM_T, NT)
            ln_full(sc, XB, XC, XC, M_T, Q_T, RS_T, NM_T, NT)
            gb = GAMB[:].unsqueeze(1).broadcast_to([128, NT, DM])
            bb = BETB[:].unsqueeze(1).broadcast_to([128, NT, DM])
            ob_ = OUTB[:].unsqueeze(1).broadcast_to([128, NT, DM])
            sc.stage(
                "vector",
                lambda e, gb=gb: e.tensor_tensor(out=XC[:], in0=XC[:], in1=gb,
                                                 op=ALU.mult),
                lambda e, bb=bb: e.tensor_tensor(out=XC[:], in0=XC[:], in1=bb,
                                                 op=ALU.add),
                lambda e: e.tensor_add(XB[:], XB[:], XA[:]),
                lambda e, ob_=ob_: e.tensor_tensor(out=XB[:], in0=XB[:], in1=ob_,
                                                   op=ALU.add),
            )
            sc.dma(lambda e, k=k: e.dma_start(out=pa2_t[k], in_=XB[:]), store=True)
            sc.stage("vector", lambda e, par=par: e.tensor_copy(
                UTAIL[:], URAW[par][:, :, TOK2 - 1:TOK2]))
            for db in range(8):
                sc.stage("tensor", *[
                    (lambda e, db=db, tb=tb: e.transpose(
                        PT[:, tb * 128:(tb + 1) * 128],
                        XC[:, tb, db * 128:(db + 1) * 128], IDENT[:]))
                    for tb in range(NT)])
                sc.stage("scalar", lambda e, db=db: e.activation(
                    out=XT[:, db, :], in_=PT[:, 0:NT * 128], func=AF.Copy))
            for grp, dst in ((0, None), (1, None), (2, None)):
                for obl in range(8):
                    po = grp * 8 + obl
                    sc.stage("tensor", *[
                        (lambda e, po=po, kb=kb: e.matmul(
                            MM0[:, 0:TOK2], INW[:, kb, po * 128:(po + 1) * 128],
                            XT[:, kb, :], start=(kb == 0), stop=(kb == 7)))
                        for kb in range(8)])
                    if grp == 0:
                        sc.stage("vector", lambda e, obl=obl, po=po, par=par:
                                 e.tensor_scalar(out=URAW[par][:, obl, :],
                                                 in0=MM0[:, 0:TOK2],
                                                 scalar1=INBS[:, po:po + 1],
                                                 scalar2=None, op0=ALU.add))
                    else:
                        sc.stage("vector", lambda e, obl=obl, po=po:
                                 e.tensor_scalar(out=CVT[:, obl, :],
                                                 in0=MM0[:, 0:TOK2],
                                                 scalar1=INBS[:, po:po + 1],
                                                 scalar2=None, op0=ALU.add))
                if grp == 1:
                    sc.stage("scalar", lambda e: e.activation(
                        out=SCR[:], in_=flat(CVT), func=AF.Exp, scale=-1.0))
                    sc.stage(
                        "vector",
                        lambda e: e.tensor_scalar_add(SCR[:], SCR[:], 1.0),
                        lambda e: e.reciprocal(out=SCR[:], in_=SCR[:]),
                        lambda e, par=par: e.tensor_tensor(
                            out=flat(SILV[par]), in0=flat(CVT), in1=SCR[:],
                            op=ALU.mult),
                    )
                elif grp == 2:
                    sc.stage("scalar", lambda e: e.activation(
                        out=SCR[:], in_=flat(CVT), func=AF.Exp, scale=-1.0))
                    sc.stage(
                        "vector",
                        lambda e: e.tensor_scalar_add(SCR[:], SCR[:], 1.0),
                        lambda e: e.reciprocal(out=SCR[:], in_=SCR[:]),
                        lambda e, par=par: e.tensor_copy(flat(SGG[par]), SCR[:]),
                    )
            if k > 0:
                attnb_tail(k - 1, URAW[par])
        attnb_tail(NCH2 - 1, None)

        sc.finish()
        run_streams(nc, sc)
    return nc


def build_l3():
    nc = bass.Bass("TRN2", target_bir_lowering=False)
    HL = L // 2
    din = {}
    def I(name, shape):
        din[name] = nc.dram_tensor(name, shape, F32, kind="ExternalInput").ap()
        return din[name]

    I("pa2", [HL, DM]); I("pb20", [HL, DM]); I("pb21", [HL, DM])
    I("lng_rep", [128, DM]); I("lnb_rep", [128, DM])
    dr_out = nc.dram_tensor("outh", [HL, DM], F32, kind="ExternalOutput").ap()

    NT = 4
    TOKL = 512
    t_in = {nm: din[nm].rearrange("(k tb p) d -> k p tb d", p=128, tb=NT)
            for nm in ("pa2", "pb20", "pb21")}
    out_t = dr_out.rearrange("(k tb p) d -> k p tb d", p=128, tb=NT)

    with contextlib.ExitStack() as ctx:
        sb = lambda nm, shp, dt: ctx.enter_context(nc.sbuf_tensor(nm, shp, dt))
        sems = {nm: ctx.enter_context(nc.semaphore(nm))
                for nm in ("ld", "st", *ENGS)}
        sc = Sched(sems)
        CONSTS = sb("consts3", [128, 2], F32)
        setup_consts(nc, sc, CONSTS)

        LNG = sb("lng3", [128, DM], F32)
        LNB = sb("lnb3", [128, DM], F32)
        XA = sb("xa3", [128, NT, DM], F32)
        XB = sb("xb3", [128, NT, DM], F32)
        XC = sb("xc3", [128, NT, DM], F32)
        M_T = sb("m_t3", [128, NT], F32)
        Q_T = sb("q_t3", [128, NT], F32)
        RS_T = sb("rs_t3", [128, NT], F32)
        NM_T = sb("nm_t3", [128, NT], F32)

        sc.dma(lambda e: e.dma_start(out=LNG[:], in_=din["lng_rep"]))
        sc.dma(lambda e: e.dma_start(out=LNB[:], in_=din["lnb_rep"]))

        for k in range(HL // TOKL):
            sc.dma(lambda e, k=k: e.dma_start(out=XA[:], in_=t_in["pa2"][k]))
            sc.dma(lambda e, k=k: e.dma_start(out=XB[:], in_=t_in["pb20"][k]))
            sc.stage("vector", lambda e: e.tensor_add(XA[:], XA[:], XB[:]))
            sc.dma(lambda e, k=k: e.dma_start(out=XB[:], in_=t_in["pb21"][k]))
            sc.stage("vector", lambda e: e.tensor_add(XA[:], XA[:], XB[:]))
            ln_full(sc, XA, XB, XC, M_T, Q_T, RS_T, NM_T, NT)
            ln_full(sc, XB, XC, XC, M_T, Q_T, RS_T, NM_T, NT)
            lg = LNG[:].unsqueeze(1).broadcast_to([128, NT, DM])
            lb = LNB[:].unsqueeze(1).broadcast_to([128, NT, DM])
            sc.stage(
                "vector",
                lambda e, lg=lg: e.tensor_tensor(out=XC[:], in0=XC[:], in1=lg,
                                                 op=ALU.mult),
                lambda e, lb=lb: e.tensor_tensor(out=XC[:], in0=XC[:], in1=lb,
                                                 op=ALU.add),
                lambda e: e.tensor_add(XA[:], XA[:], XC[:]),
            )
            sc.dma(lambda e, k=k: e.dma_start(out=out_t[k], in_=XA[:]), store=True)

        sc.finish()
        run_streams(nc, sc)
    return nc


def _tile_w(w, kblocks):
    K, M = w.shape
    assert K == kblocks * 128
    return np.ascontiguousarray(w.reshape(kblocks, 128, M).transpose(1, 0, 2))


def _tile_b(v, nblocks):
    return np.ascontiguousarray(v.reshape(nblocks, 128).T)


def _rep(v):
    return np.ascontiguousarray(np.broadcast_to(np.asarray(v, np.float32),
                                                (128, DM)))


def stage_l1_maps(inp):
    f = lambda x: np.ascontiguousarray(np.asarray(x, np.float32))
    ident = np.eye(128, dtype=np.float32)
    x = f(inp["L0_attn_a_x"])
    W_up = f(inp["L0_attn_a_W_up"]); b_up = f(inp["L0_attn_a_b_up"])
    conv_w = f(inp["L0_attn_a_conv_w"]); conv_b = f(inp["L0_attn_a_conv_b"])
    W_delta = f(inp["L0_attn_a_W_delta"]); b_delta = f(inp["L0_attn_a_b_delta"])
    W_B = f(inp["L0_attn_a_W_B"]); W_C = f(inp["L0_attn_a_W_C"])
    A_log = f(inp["L0_attn_a_A_log"]); Dv = f(inp["L0_attn_a_D"])
    W_down = f(inp["L0_attn_a_W_down"]); b_down = f(inp["L0_attn_a_b_down"])
    gam = f(inp["L0_attn_a_ln_gamma"]); bet = f(inp["L0_attn_a_ln_beta"])

    maps1 = []
    for c in range(8):
        b, j = divmod(c, 2)
        my = slice(j * CHA, (j + 1) * CHA)
        perm = np.concatenate([np.arange(j * CHA, (j + 1) * CHA),
                               np.arange((1 - j) * CHA, (1 - j) * CHA + CHA)])
        maps1.append(dict(
            x0=np.ascontiguousarray(x[b]),
            gam_rep=_rep(gam), bet_rep=_rep(bet),
            bdcol=_tile_b(b_down, 8) if j == 0 else np.zeros((128, 8), np.float32),
            wup_l=_tile_w(W_up[:, 0:DI][:, perm], 8),
            wup_r=_tile_w(W_up[:, DI + j * CHA: DI + (j + 1) * CHA], 8),
            bup_l=_tile_b(b_up[0:DI][perm], 4),
            bup_r=_tile_b(b_up[DI + j * CHA: DI + (j + 1) * CHA], 2),
            convw=np.ascontiguousarray(
                conv_w[perm, 0, :].reshape(4, 128, 3).transpose(1, 0, 2)),
            convb=_tile_b(conv_b[perm], 4),
            wdelta=_tile_w(W_delta[perm][:, my], 4),
            bdelta=_tile_b(b_delta[my], 2),
            wb=_tile_w(W_B[perm], 4),
            wc=_tile_w(W_C[perm], 4),
            alog=np.ascontiguousarray(
                A_log[my].reshape(2, 128, NS).transpose(1, 0, 2)),
            dvec=_tile_b(Dv[my], 2),
            wdown=_tile_w(W_down[my], 2),
            ident=ident,
        ))
    return maps1


def stage_l2_maps(inp, r1):
    f = lambda x: np.ascontiguousarray(np.asarray(x, np.float32))
    ident = np.eye(128, dtype=np.float32)
    in_w = f(inp["L0_attn_b_in_w"]); in_b = f(inp["L0_attn_b_in_b"])
    conv_w2 = f(inp["L0_attn_b_conv_w"]); conv_b2 = f(inp["L0_attn_b_conv_b"])
    A_logb = f(inp["L0_attn_b_A_log"]); Bpv = f(inp["L0_attn_b_B_p"])
    Cpv = f(inp["L0_attn_b_C_p"]); Dpv = f(inp["L0_attn_b_D_p"])
    out_w = f(inp["L0_attn_b_out_w"]); out_b = f(inp["L0_attn_b_out_b"])
    gamb = f(inp["L0_attn_b_norm_gamma"]); betb = f(inp["L0_attn_b_norm_beta"])
    lng = f(inp["L0_ffn_ln_g"]); lnb = f(inp["L0_ffn_ln_b"])

    maps2 = []
    for c in range(8):
        b, j = divmod(c, 2)
        myb = slice(j * CHB, (j + 1) * CHB)
        inw_cat = np.concatenate([in_w[:, j * CHB:(j + 1) * CHB],
                                  in_w[:, INB + j * CHB: INB + (j + 1) * CHB],
                                  in_w[:, 2 * INB + j * CHB: 2 * INB + (j + 1) * CHB]],
                                 axis=1)
        inb_cat = np.concatenate([in_b[j * CHB:(j + 1) * CHB],
                                  in_b[INB + j * CHB: INB + (j + 1) * CHB],
                                  in_b[2 * INB + j * CHB: 2 * INB + (j + 1) * CHB]])
        maps2.append(dict(
            parta=r1[2 * b]["parta"],
            partb0=r1[2 * b]["partb"],
            partb1=r1[2 * b + 1]["partb"],
            lng_rep=_rep(lng), lnb_rep=_rep(lnb),
            gamb_rep=_rep(gamb), betb_rep=_rep(betb),
            inw=_tile_w(inw_cat, 8),
            inb=_tile_b(inb_cat, 24),
            convw2=np.ascontiguousarray(
                conv_w2[myb, 0, :].reshape(8, 128, 3).transpose(1, 0, 2)),
            convb2=_tile_b(conv_b2[myb], 8),
            alogb=np.ascontiguousarray(
                A_logb[myb].reshape(8, 128, DSB).transpose(1, 0, 2)),
            bp=np.ascontiguousarray(
                Bpv[myb].reshape(8, 128, DSB).transpose(1, 0, 2)),
            cp=np.ascontiguousarray(
                Cpv[myb].reshape(8, 128, DSB).transpose(1, 0, 2)),
            dp=_tile_b(Dpv[myb], 8),
            outw=_tile_w(out_w[myb], 8),
            outb_rep=_rep(out_b) if j == 0 else np.zeros((128, DM), np.float32),
            ident=ident,
        ))
    return maps2


def stage_l3_maps(inp, r2):
    f = lambda x: np.ascontiguousarray(np.asarray(x, np.float32))
    lng = f(inp["L0_ffn_ln_g"]); lnb = f(inp["L0_ffn_ln_b"])
    HL = L // 2
    maps3 = []
    for c in range(8):
        b, j = divmod(c, 2)
        rows = slice(j * HL, (j + 1) * HL)
        maps3.append(dict(
            pa2=np.ascontiguousarray(r2[2 * b]["parta2"][rows]),
            pb20=np.ascontiguousarray(r2[2 * b]["partb2"][rows]),
            pb21=np.ascontiguousarray(r2[2 * b + 1]["partb2"][rows]),
            lng_rep=_rep(lng), lnb_rep=_rep(lnb),
        ))
    return maps3


def kernel(**inp):
    maps1 = stage_l1_maps(inp)
    nc1 = build_l1()
    r1 = run_bass_kernel_spmd(nc1, maps1, core_ids=list(range(8))).results
    maps2 = stage_l2_maps(inp, r1)
    nc2 = build_l2()
    r2 = run_bass_kernel_spmd(nc2, maps2, core_ids=list(range(8))).results
    maps3 = stage_l3_maps(inp, r2)
    nc3 = build_l3()
    r3 = run_bass_kernel_spmd(nc3, maps3, core_ids=list(range(8))).results

    out = np.empty((B, L, DM), np.float32)
    hl = L // 2
    for c in range(8):
        b, j = divmod(c, 2)
        out[b, j * hl:(j + 1) * hl] = r3[c]["outh"]
    return out


# version 04
# speedup vs baseline: 1.7915x; 1.7915x over previous
import contextlib
import numpy as np

import concourse.bass as bass
import concourse.mybir as mybir
from concourse.bass_utils import run_bass_kernel_spmd

F32 = mybir.dt.float32
F16 = mybir.dt.float16
BF16 = mybir.dt.bfloat16
AF = mybir.ActivationFunctionType
ALU = mybir.AluOpType
AX = mybir.AxisListType

B, L, DM = 4, 8192, 1024
DI, NS = 512, 256
CHA = 256
INB, DSB = 2048, 64
CHB = 1024
TAPS = 64
EPS = 1e-5

TOK = 512
TS = 1024
NG = L // TS
KPG = TS // TOK

TOK2 = 256
NCH2 = L // TOK2

ENGS = ("sync", "vector", "scalar", "tensor", "gpsimd")


class Sched:

    def __init__(self, sems):
        self.sems = sems
        self.ops = {e: [] for e in ENGS}
        self.cnt = {e: 0 for e in ENGS}
        self.ld = 0
        self.st = 0
        self.last = None

    def _wait_last(self, eng):
        if self.last is None:
            return
        nm, v = self.last
        self.ops[eng].append(lambda e, nm=nm, v=v: e.wait_ge(self.sems[nm], v))

    def _emit(self, eng, fns, chain, lag=1):
        for fn in fns:
            if chain and self.cnt[eng] > 0:
                v = self.cnt[eng] - (lag - 1)
                if v > 0:
                    self.ops[eng].append(
                        lambda e, nm=eng, v=v: e.wait_ge(self.sems[nm], v))
            self.cnt[eng] += 1
            def run(e, fn=fn, nm=eng):
                ins = fn(e)
                ins.then_inc(self.sems[nm], 1)
            self.ops[eng].append(run)
        self.last = (eng, self.cnt[eng])
        return self.last

    def stage(self, eng, *fns, lag=1):
        assert fns
        self._wait_last(eng)
        return self._emit(eng, fns, True, lag)

    def stage_after(self, eng, events, *fns, chain=True):
        assert fns
        for nm, v in events:
            if v > 0:
                self.ops[eng].append(
                    lambda e, nm=nm, v=v: e.wait_ge(self.sems[nm], v))
        return self._emit(eng, fns, chain)

    def dma(self, fn, store=False, events=()):
        self._wait_last("sync")
        for nm, v in events:
            if v > 0:
                self.ops["sync"].append(
                    lambda e, nm=nm, v=v: e.wait_ge(self.sems[nm], v))
        if store:
            self.st += 16
            nm, v = "st", self.st
        else:
            self.ld += 16
            nm, v = "ld", self.ld
        def run(e, fn=fn, nm=nm):
            fn(e).then_inc(self.sems[nm], 16)
        self.ops["sync"].append(run)
        self.last = (nm, v)

    def finish(self):
        self._wait_last("sync")
        if self.st:
            self.ops["sync"].append(lambda e: e.wait_ge(self.sems["st"], self.st))


def run_streams(nc, sc):
    with nc.Block() as block:
        @block.sync
        def _(e):
            for f in sc.ops["sync"]:
                f(e)

        @block.vector
        def _(e):
            for f in sc.ops["vector"]:
                f(e)

        @block.scalar
        def _(e):
            for f in sc.ops["scalar"]:
                f(e)

        @block.tensor
        def _(e):
            for f in sc.ops["tensor"]:
                f(e)

        @block.gpsimd
        def _(e):
            for f in sc.ops["gpsimd"]:
                f(e)


def flat(ap):
    return ap.rearrange("p a b -> p (a b)")


def ln_full(sc, x, zout, sq, m_t, q_t, rs_t, nm_t, nt, dm=DM):
    sc.stage("scalar", lambda e: e.activation(out=sq[:], in_=x[:], func=AF.Square))
    sc.stage(
        "vector",
        lambda e: e.tensor_reduce(out=q_t[:], in_=sq[:], axis=AX.X, op=ALU.add),
        lambda e: e.tensor_reduce(out=m_t[:], in_=x[:], axis=AX.X, op=ALU.add),
        lambda e: e.tensor_scalar_mul(m_t[:], m_t[:], 1.0 / dm),
        lambda e: e.tensor_scalar_mul(q_t[:], q_t[:], 1.0 / dm),
        lambda e: e.tensor_mul(rs_t[:], m_t[:], m_t[:]),
        lambda e: e.tensor_sub(q_t[:], q_t[:], rs_t[:]),
    )
    sc.stage(
        "scalar",
        lambda e: e.activation(out=rs_t[:], in_=q_t[:], func=AF.Ln, bias=EPS),
        lambda e: e.activation(out=rs_t[:], in_=rs_t[:], func=AF.Exp, scale=-0.5),
    )
    fns = [lambda e: e.tensor_mul(nm_t[:], m_t[:], rs_t[:])]
    for tb in range(nt):
        fns.append(lambda e, tb=tb: e.tensor_scalar(
            out=zout[:, tb, :], in0=x[:, tb, :],
            scalar1=rs_t[:, tb:tb + 1], scalar2=nm_t[:, tb:tb + 1],
            op0=ALU.mult, op1=ALU.subtract))
    sc.stage("vector", *fns)


def sigmoid_chain(sc, src, dst, scr, n):
    sc.stage("scalar", lambda e: e.activation(out=scr, in_=src, func=AF.Exp, scale=-1.0))
    sc.stage(
        "vector",
        lambda e: e.tensor_scalar_add(scr, scr, 1.0),
        lambda e: e.reciprocal(out=scr, in_=scr),
        lambda e: e.tensor_copy(dst, scr),
    )



def setup_consts(nc, sc, consts):
    sc.stage(
        "vector",
        lambda e: e.memset(consts[:, 0:1], EPS),
        lambda e: e.memset(consts[:, 1:2], 1.0),
    )
    nc.const_aps.register(F32, EPS, consts[:, 0:1])
    nc.const_aps.register(F32, 1.0, consts[:, 1:2])

def build_l1():
    nc = bass.Bass("TRN2", target_bir_lowering=False)

    din = {}
    def I(name, shape):
        din[name] = nc.dram_tensor(name, shape, F32, kind="ExternalInput").ap()
        return din[name]

    dr_x0 = I("x0", [L, DM])
    I("gam_rep", [128, DM]); I("bet_rep", [128, DM]); I("bdcol", [128, 8])
    I("wup_l", [128, 8, DI]); I("wup_r", [128, 8, CHA])
    I("bup_l", [128, 4]); I("bup_r", [128, 2])
    I("convw", [128, 4, 3]); I("convb", [128, 4])
    I("wdelta", [128, 4, CHA]); I("bdelta", [128, 2])
    I("wb", [128, 4, NS]); I("wc", [128, 4, NS])
    I("alog", [128, 2, NS]); I("dvec", [128, 2])
    I("wdown", [128, 2, DM]); I("ident", [128, 128])

    dr_parta = nc.dram_tensor("parta", [L, DM], F32, kind="ExternalOutput").ap()
    dr_partb = nc.dram_tensor("partb", [L, DM], F32, kind="ExternalOutput").ap()

    x0_t = dr_x0.rearrange("(k tb p) d -> k p tb d", p=128, tb=TOK // 128)
    pa_t = dr_parta.rearrange("(k tb p) d -> k p tb d", p=128, tb=TOK // 128)
    pb_t = dr_partb.rearrange("(k tb p) d -> k p tb d", p=128, tb=TOK // 128)

    with contextlib.ExitStack() as ctx:
        sb = lambda nm, shp, dt: ctx.enter_context(nc.sbuf_tensor(nm, shp, dt))
        ps = lambda nm, shp: ctx.enter_context(nc.psum_tensor(nm, shp, F32))
        sems = {nm: ctx.enter_context(nc.semaphore(nm))
                for nm in ("ld", "st", *ENGS)}
        sc = Sched(sems)
        CONSTS = sb("consts", [128, 2], F32)
        setup_consts(nc, sc, CONSTS)

        W_UPL = sb("w_upl", [128, 8, DI], F16)
        W_UPR = sb("w_upr", [128, 8, CHA], F16)
        W_DEL = sb("w_del", [128, 4, CHA], F16)
        W_B = sb("w_b", [128, 4, NS], F16)
        W_C = sb("w_c", [128, 4, NS], F16)
        W_DN = sb("w_dn", [128, 2, DM], F16)
        GAM = sb("gam", [128, DM], F32)
        BET = sb("bet", [128, DM], F32)
        BDCOL = sb("bdcol_s", [128, 8], F32)
        BUPL = sb("bupl", [128, 4], F32)
        BUPR = sb("bupr", [128, 2], F32)
        CONVW = sb("convw_s", [128, 4, 3], F32)
        CONVB = sb("convb_s", [128, 4], F32)
        BDEL = sb("bdel", [128, 2], F32)
        A_SB = sb("a_sb", [128, 2, NS], F32)
        DVEC = sb("dvec_s", [128, 2], F32)
        IDENT = sb("ident_s", [128, 128], F32)
        IDENTB = sb("identb", [128, 128], F16)

        X0 = sb("x0t", [128, 4, DM], F32)
        YN = sb("yn", [128, 4, DM], F32)
        YT = sb("yt", [128, 8, TOK], F16)
        LEFTX = sb("leftx", [128, 4, TOK + 2], F32)
        RIGHT = sb("right", [128, 2, TOK], F32)
        CONVT = sb("convt", [128, 4, TOK], F32)
        DRAW = sb("draw", [128, 2, TOK], F32)
        U_BF = sb("u_bf", [128, 4, TOK], F16)
        M_T = sb("m_t", [128, 4], F32)
        Q_T = sb("q_t", [128, 4], F32)
        RS_T = sb("rs_t", [128, 4], F32)
        NM_T = sb("nm_t", [128, 4], F32)

        DELTA = sb("delta", [128, 2, TS], F32)
        DELU = sb("delu", [128, 2, TS], F16)
        BMT = sb("bmt", [128, 2, TS], F16)
        CMT = sb("cmt", [128, 2, TS], F16)
        SILR = sb("silr", [128, 2, TS], F16)
        YACC = sb("yacc", [128, 2, TS], F32)
        UDT = sb("udt", [128, 2, TS], BF16)
        CARRY = sb("carry", [128, 2, NS], F32)
        IDENT16 = sb("ident16", [128, 128], BF16)
        SQT = sb("sqt", [128, 4, DM], F32)

        DA = [sb("da0", [128, 2, TS], F32), sb("da1", [128, 2, TS], F32)]
        BMBC = [sb("bmbc0", [128, TS], F16), sb("bmbc1", [128, TS], F16)]
        CCBC = [sb("ccbc0", [128, TS], BF16), sb("ccbc1", [128, TS], BF16)]
        DAC1 = [sb("dac1a", [128, 1], F32), sb("dac1b", [128, 1], F32)]
        D1 = [sb("d1a", [128, 2, TS], F16), sb("d1b", [128, 2, TS], F16)]
        SS = [sb("ss0", [128, 2, TS], BF16), sb("ss1", [128, 2, TS], BF16)]

        P8 = ps("p8", [128, 4096])
        YPS = P8[:, 0:2048]
        BCPB = P8[:, 2048:3072]
        BCPC = P8[:, 3072:4096]
        MM0 = P8[:, 2048:2560]
        PT = P8[:, 3584:4096]

        casts = (("wup_l", W_UPL), ("wup_r", W_UPR), ("wdelta", W_DEL),
                 ("wb", W_B), ("wc", W_C), ("wdown", W_DN))
        for nm, dst in casts:
            n = int(np.prod(din[nm].shape[1:]))
            scr = flat(YN)[:, 0:n]
            sc.dma(lambda e, nm=nm, scr=scr: e.dma_start(out=scr, in_=flat(din[nm])))
            sc.stage("vector", lambda e, dst=dst, scr=scr: e.tensor_copy(flat(dst), scr))
        for nm, dst in (("gam_rep", GAM), ("bet_rep", BET), ("bdcol", BDCOL),
                        ("bup_l", BUPL), ("bup_r", BUPR), ("convw", CONVW),
                        ("convb", CONVB), ("bdelta", BDEL), ("alog", A_SB),
                        ("dvec", DVEC), ("ident", IDENT)):
            sc.dma(lambda e, nm=nm, dst=dst: e.dma_start(out=dst[:], in_=din[nm]))
        sc.stage(
            "vector",
            lambda e: e.tensor_copy(IDENTB[:], IDENT[:]),
            lambda e: e.tensor_copy(IDENT16[:], IDENT[:]),
            lambda e: e.memset(CARRY[:], 0.0),
            lambda e: e.memset(LEFTX[:, :, 0:2], 0.0),
            lambda e: e.memset(DA[0][:, 1, 0:1], 0.0),
            lambda e: e.memset(DA[1][:, 1, 0:1], 0.0),
        )
        sc.stage("scalar", lambda e: e.activation(out=A_SB[:], in_=A_SB[:], func=AF.Exp))
        sc.stage("vector", lambda e: e.tensor_scalar_mul(A_SB[:], A_SB[:], -1.0))

        pebc_ev, act_ev, dve_ev, pool_ev, pey_ev = {}, {}, {}, {}, {}
        act_start = {}
        for g in range(NG):
            for kk in range(KPG):
                k = g * KPG + kk
                ksl = slice(kk * TOK, (kk + 1) * TOK)

                sc.dma(lambda e, k=k: e.dma_start(out=X0[:], in_=x0_t[k]),
                       events=[("st", sc.st)])

                ln_full(sc, X0, YN, SQT, M_T, Q_T, RS_T, NM_T, 4)
                sc.stage("vector", lambda e: e.tensor_add(X0[:], X0[:], YN[:]))
                sc.dma(lambda e, k=k: e.dma_start(out=pa_t[k], in_=X0[:]), store=True)
                ln_full(sc, YN, YN, SQT, M_T, Q_T, RS_T, NM_T, 4)
                gm = GAM[:].unsqueeze(1).broadcast_to([128, 4, DM])
                bt = BET[:].unsqueeze(1).broadcast_to([128, 4, DM])
                sc.stage(
                    "vector",
                    lambda e, gm=gm: e.tensor_tensor(out=YN[:], in0=YN[:], in1=gm,
                                                     op=ALU.mult),
                    lambda e, bt=bt: e.tensor_tensor(out=YN[:], in0=YN[:], in1=bt,
                                                     op=ALU.add),
                )

                for db in range(8):
                    sc.stage("tensor", *[
                        (lambda e, db=db, tb=tb: e.transpose(
                            PT[:, tb * 128:(tb + 1) * 128],
                            YN[:, tb, db * 128:(db + 1) * 128], IDENT[:]))
                        for tb in range(4)])
                    sc.stage("scalar", lambda e, db=db: e.activation(
                        out=YT[:, db, :], in_=PT[:], func=AF.Copy))

                for i, (wt, nob) in enumerate(((W_UPL, 4), (W_UPR, 2))):
                    for ob in range(nob):
                        sc.stage("tensor", *[
                            (lambda e, wt=wt, kb=kb, ob=ob: e.matmul(
                                MM0[:], wt[:, kb, ob * 128:(ob + 1) * 128],
                                YT[:, kb, :], start=(kb == 0), stop=(kb == 7)))
                            for kb in range(8)])
                        if i == 0:
                            sc.stage("vector", lambda e, ob=ob: e.tensor_scalar(
                                out=LEFTX[:, ob, 2:2 + TOK], in0=MM0[:],
                                scalar1=BUPL[:, ob:ob + 1], scalar2=None,
                                op0=ALU.add))
                        else:
                            sc.stage("vector", lambda e, ob=ob: e.tensor_scalar(
                                out=RIGHT[:, ob, :], in0=MM0[:],
                                scalar1=BUPR[:, ob:ob + 1], scalar2=None,
                                op0=ALU.add))

                convfns = []
                for ob in range(4):
                    convfns.append(lambda e, ob=ob: e.tensor_scalar(
                        out=CONVT[:, ob, :], in0=LEFTX[:, ob, 2:2 + TOK],
                        scalar1=CONVW[:, ob, 2:3], scalar2=None, op0=ALU.mult))
                    convfns.append(lambda e, ob=ob: e.scalar_tensor_tensor(
                        out=CONVT[:, ob, :], in0=LEFTX[:, ob, 1:1 + TOK],
                        scalar=CONVW[:, ob, 1:2], in1=CONVT[:, ob, :],
                        op0=ALU.mult, op1=ALU.add))
                    convfns.append(lambda e, ob=ob: e.scalar_tensor_tensor(
                        out=CONVT[:, ob, :], in0=LEFTX[:, ob, 0:TOK],
                        scalar=CONVW[:, ob, 0:1], in1=CONVT[:, ob, :],
                        op0=ALU.mult, op1=ALU.add))
                convfns.append(lambda e: e.tensor_copy(LEFTX[:, :, 0:2],
                                                       LEFTX[:, :, TOK:TOK + 2]))
                sc.stage("vector", *convfns)
                sc.stage("vector", *[
                    (lambda e, ob=ob: e.tensor_scalar(
                        out=CONVT[:, ob, :], in0=CONVT[:, ob, :],
                        scalar1=CONVB[:, ob:ob + 1], scalar2=None, op0=ALU.add))
                    for ob in range(4)])
                sc.stage("scalar", lambda e: e.activation(
                    out=flat(YN)[:, 0:4 * TOK], in_=flat(CONVT), func=AF.Exp,
                    scale=-1.0))
                ynf = flat(YN)
                sc.stage(
                    "vector",
                    lambda e: e.tensor_scalar_add(ynf[:, 0:4 * TOK],
                                                  ynf[:, 0:4 * TOK], 1.0),
                    lambda e: e.reciprocal(out=ynf[:, 0:4 * TOK],
                                           in_=ynf[:, 0:4 * TOK]),
                    lambda e: e.tensor_tensor(out=flat(CONVT), in0=flat(CONVT),
                                              in1=ynf[:, 0:4 * TOK], op=ALU.mult),
                    lambda e: e.tensor_copy(U_BF[:], CONVT[:]),
                    lambda e, ksl=ksl: e.tensor_scalar(
                        out=UDT[:, 0, ksl], in0=CONVT[:, 0, :],
                        scalar1=DVEC[:, 0:1], scalar2=None, op0=ALU.mult),
                    lambda e, ksl=ksl: e.tensor_scalar(
                        out=UDT[:, 1, ksl], in0=CONVT[:, 1, :],
                        scalar1=DVEC[:, 1:2], scalar2=None, op0=ALU.mult),
                )

                sc.stage("scalar", lambda e: e.activation(
                    out=ynf[:, 0:2 * TOK], in_=flat(RIGHT), func=AF.Exp, scale=-1.0))
                srl = SILR[:, :, ksl]
                yn3 = ynf[:, 0:2 * TOK].rearrange("p (a b) -> p a b", a=2)
                sc.stage(
                    "vector",
                    lambda e: e.tensor_scalar_add(ynf[:, 0:2 * TOK],
                                                  ynf[:, 0:2 * TOK], 1.0),
                    lambda e: e.reciprocal(out=ynf[:, 0:2 * TOK],
                                           in_=ynf[:, 0:2 * TOK]),
                    lambda e, srl=srl, yn3=yn3: e.tensor_tensor(
                        out=srl, in0=RIGHT[:], in1=yn3, op=ALU.mult),
                )

                for pb in range(2):
                    sc.stage("tensor", *[
                        (lambda e, pb=pb, kb=kb: e.matmul(
                            MM0[:], W_DEL[:, kb, pb * 128:(pb + 1) * 128],
                            U_BF[:, kb, :], start=(kb == 0), stop=(kb == 3)))
                        for kb in range(4)])
                    sc.stage("vector", lambda e, pb=pb: e.tensor_scalar(
                        out=DRAW[:, pb, :], in0=MM0[:],
                        scalar1=BDEL[:, pb:pb + 1], scalar2=None, op0=ALU.add))
                for wt, dstt in ((W_B, BMT), (W_C, CMT)):
                    for pb in range(2):
                        sc.stage("tensor", *[
                            (lambda e, wt=wt, pb=pb, kb=kb: e.matmul(
                                MM0[:], wt[:, kb, pb * 128:(pb + 1) * 128],
                                U_BF[:, kb, :], start=(kb == 0), stop=(kb == 3)))
                            for kb in range(4)])
                        sc.stage("scalar", lambda e, dstt=dstt, pb=pb, ksl=ksl:
                                 e.activation(out=dstt[:, pb, ksl], in_=MM0[:],
                                              func=AF.Copy))

                drf = DRAW[:]
                dsl = DELTA[:, :, ksl]
                yn3 = ynf[:, 0:2 * TOK].rearrange("p (a b) -> p a b", a=2)
                sc.stage(
                    "scalar",
                    lambda e, dsl=dsl: e.activation(out=dsl, in_=drf, func=AF.Relu),
                    lambda e, yn3=yn3: e.activation(out=yn3, in_=drf, func=AF.Abs),
                    lambda e: e.activation(out=ynf[:, 0:2 * TOK],
                                           in_=ynf[:, 0:2 * TOK], func=AF.Exp,
                                           scale=-1.0),
                    lambda e: e.activation(out=ynf[:, 0:2 * TOK],
                                           in_=ynf[:, 0:2 * TOK], func=AF.Ln,
                                           bias=1.0),
                )
                dusl = DELU[:, :, ksl]
                cu2 = CONVT[:, 0:2, :]
                sc.stage(
                    "vector",
                    lambda e, dsl=dsl, yn3=yn3: e.tensor_tensor(
                        out=dsl, in0=dsl, in1=yn3, op=ALU.add),
                    lambda e, dusl=dusl, dsl=dsl, cu2=cu2: e.tensor_tensor(
                        out=dusl, in0=dsl, in1=cu2, op=ALU.mult),
                )

            tok_end = sc.last
            for n in range(NS):
                it = g * NS + n
                buf = it % 2
                nb, nn = divmod(n, 128)
                sel = IDENTB[:, nn:nn + 1].broadcast_to([128, 128])

                if it >= 2:
                    pbuf = (it - 2) % 2
                    ssf_p = flat(SS[pbuf])
                    pey_ev[it - 2] = sc.stage_after(
                        "tensor", [dve_ev[it - 2]], *[
                            (lambda e, ssf_p=ssf_p, s4=s4: e.matmul(
                                YPS[:, s4 * 512:(s4 + 1) * 512], IDENT16[:],
                                ssf_p[:, s4 * 512:(s4 + 1) * 512],
                                start=False, stop=True))
                            for s4 in range(4)], chain=False)


                pe_fns = []
                pe_evs = []
                if n == 0:
                    pe_evs.append(tok_end)
                    udf = flat(UDT)
                    for s4 in range(4):
                        pe_fns.append(lambda e, s4=s4: e.matmul(
                            YPS[:, s4 * 512:(s4 + 1) * 512], IDENT16[:],
                            udf[:, s4 * 512:(s4 + 1) * 512],
                            start=True, stop=True))
                if it >= 1:
                    pe_evs.append(("scalar", act_start[it - 1] + 2))
                for s2 in range(2):
                    pe_fns.append(lambda e, sel=sel, nb=nb, s2=s2: e.matmul(
                        BCPB[:, s2 * 512:(s2 + 1) * 512], sel,
                        BMT[:, nb, s2 * 512:(s2 + 1) * 512],
                        start=True, stop=True))
                    pe_fns.append(lambda e, sel=sel, nb=nb, s2=s2: e.matmul(
                        BCPC[:, s2 * 512:(s2 + 1) * 512], sel,
                        CMT[:, nb, s2 * 512:(s2 + 1) * 512],
                        start=True, stop=True))
                pebc_ev[it] = sc.stage_after("tensor", pe_evs, *pe_fns,
                                             chain=False)

                act_evs = [pebc_ev[it]]
                if it >= 2:
                    act_evs.append(dve_ev[it - 2])
                act_start[it] = sc.cnt["scalar"]
                act_ev[it] = sc.stage_after(
                    "scalar", act_evs,
                    lambda e, buf=buf: e.activation(out=BMBC[buf][:], in_=BCPB,
                                                    func=AF.Copy),
                    lambda e, buf=buf: e.activation(out=CCBC[buf][:], in_=BCPC,
                                                    func=AF.Copy),
                    lambda e, buf=buf, n=n: e.activation(
                        out=DA[buf][:, 0, :], in_=DELTA[:, 0, :], func=AF.Exp,
                        scale=A_SB[:, 0, n:n + 1]),
                    lambda e, buf=buf, n=n: e.activation(
                        out=DA[buf][:, 1, 1:TS], in_=DELTA[:, 1, 1:TS],
                        func=AF.Exp, scale=A_SB[:, 1, n:n + 1]),
                    lambda e, buf=buf, n=n: e.activation(
                        out=DAC1[buf][:], in_=DELTA[:, 1, 0:1], func=AF.Exp,
                        scale=A_SB[:, 1, n:n + 1]),
                    chain=False)

                dve_evs = [act_ev[it]]
                if it >= 2:
                    dve_evs.append(pey_ev[it - 2])
                bmb = BMBC[buf][:].unsqueeze(1).broadcast_to([128, 2, TS])
                ccb = CCBC[buf][:].unsqueeze(1).broadcast_to([128, 2, TS])
                dve_ev[it] = sc.stage_after(
                    "vector", dve_evs,
                    lambda e, bmb=bmb, buf=buf: e.tensor_tensor(
                        out=D1[buf][:], in0=DELU[:], in1=bmb, op=ALU.mult),
                    lambda e, buf=buf, n=n: e.scalar_tensor_tensor(
                        out=D1[buf][:, 1, 0:1], in0=DAC1[buf][:],
                        scalar=CARRY[:, 1, n:n + 1], in1=D1[buf][:, 1, 0:1],
                        op0=ALU.mult, op1=ALU.add),
                    lambda e, buf=buf, n=n: e.tensor_tensor_scan(
                        out=flat(SS[buf]), data0=flat(DA[buf]),
                        data1=flat(D1[buf]),
                        initial=CARRY[:, 0, n:n + 1], op0=ALU.mult, op1=ALU.add),
                    lambda e, buf=buf, n=n: e.tensor_copy(
                        CARRY[:, :, n:n + 1], SS[buf][:, :, TS - 1:TS]),
                    lambda e, buf=buf, ccb=ccb: e.tensor_tensor(
                        out=SS[buf][:], in0=SS[buf][:], in1=ccb, op=ALU.mult),
                )

            itl = g * NS + NS - 1
            for itx in (itl - 1, itl):
                ssf_l = flat(SS[itx % 2])
                pey_ev[itx] = sc.stage_after("tensor", [dve_ev[itx]], *[
                    (lambda e, ssf_l=ssf_l, s4=s4: e.matmul(
                        YPS[:, s4 * 512:(s4 + 1) * 512], IDENT16[:],
                        ssf_l[:, s4 * 512:(s4 + 1) * 512], start=False,
                        stop=True))
                    for s4 in range(4)], chain=False)
            sc.last = pey_ev[itl]
            sc.stage("scalar", lambda e: e.activation(
                out=flat(YACC), in_=YPS, func=AF.Copy))

            sc.stage("vector", lambda e: e.tensor_tensor(
                out=SS[0][:], in0=YACC[:], in1=SILR[:], op=ALU.mult))
            sc.stage_after("vector", [("st", sc.st)],
                           lambda e: e.engine_nop())
            for sub in range(KPG):
                tsl = slice(sub * TOK, (sub + 1) * TOK)
                k = g * KPG + sub
                for ob in range(8):
                    sc.stage("tensor", *[
                        (lambda e, ob=ob, dblk=dblk, tsl=tsl: e.matmul(
                            MM0[:], W_DN[:, dblk, ob * 128:(ob + 1) * 128],
                            SS[0][:, dblk, tsl], start=(dblk == 0),
                            stop=(dblk == 1)))
                        for dblk in range(2)])
                    sc.stage("vector", lambda e, ob=ob: e.tensor_scalar(
                        out=flat(CONVT)[:, 0:TOK], in0=MM0[:],
                        scalar1=BDCOL[:, ob:ob + 1], scalar2=None, op0=ALU.add))
                    sc.stage("tensor", *[
                        (lambda e, tb=tb: e.transpose(
                            PT[:, tb * 128:(tb + 1) * 128],
                            flat(CONVT)[:, tb * 128:(tb + 1) * 128], IDENT[:]))
                        for tb in range(4)])
                    sc.stage("scalar", lambda e, ob=ob: e.activation(
                        out=X0[:, :, ob * 128:(ob + 1) * 128],
                        in_=PT[:].rearrange("p (tb c) -> p tb c", tb=4),
                        func=AF.Copy))
                sc.dma(lambda e, k=k: e.dma_start(out=pb_t[k], in_=X0[:]),
                       store=True)

        sc.finish()
        run_streams(nc, sc)
    return nc


def build_l2():
    nc = bass.Bass("TRN2", target_bir_lowering=False)

    din = {}
    def I(name, shape):
        din[name] = nc.dram_tensor(name, shape, F32, kind="ExternalInput").ap()
        return din[name]

    I("parta", [L, DM]); I("partb0", [L, DM]); I("partb1", [L, DM])
    I("lng_rep", [128, DM]); I("lnb_rep", [128, DM])
    I("gamb_rep", [128, DM]); I("betb_rep", [128, DM])
    I("inw", [128, 8, 3 * CHB]); I("inb", [128, 24])
    I("convw2", [128, 8, 3]); I("convb2", [128, 8])
    I("alogb", [128, 8, DSB]); I("bp", [128, 8, DSB]); I("cp", [128, 8, DSB])
    I("dp", [128, 8])
    I("outw", [128, 8, DM]); I("outb_rep", [128, DM])
    I("ident", [128, 128])

    dr_x2 = nc.dram_tensor("x2", [L, DM], F32, kind="ExternalOutput").ap()
    dr_pa2 = nc.dram_tensor("parta2", [L, DM], F32, kind="ExternalOutput").ap()
    dr_pb2 = nc.dram_tensor("partb2", [L, DM], F32, kind="ExternalOutput").ap()

    NT = TOK2 // 128
    t_in = {nm: din[nm].rearrange("(k tb p) d -> k p tb d", p=128, tb=NT)
            for nm in ("parta", "partb0", "partb1")}
    x2_t = dr_x2.rearrange("(k tb p) d -> k p tb d", p=128, tb=NT)
    pa2_t = dr_pa2.rearrange("(k tb p) d -> k p tb d", p=128, tb=NT)
    pb2_t = dr_pb2.rearrange("(k tb p) d -> k p tb d", p=128, tb=NT)

    with contextlib.ExitStack() as ctx:
        sb = lambda nm, shp, dt: ctx.enter_context(nc.sbuf_tensor(nm, shp, dt))
        ps = lambda nm, shp: ctx.enter_context(nc.psum_tensor(nm, shp, F32))
        sems = {nm: ctx.enter_context(nc.semaphore(nm))
                for nm in ("ld", "st", *ENGS)}
        sc = Sched(sems)
        CONSTS = sb("consts2", [128, 2], F32)
        setup_consts(nc, sc, CONSTS)

        INW = sb("inw_s", [128, 8, 3 * CHB], F16)
        OUTW = sb("outw_s", [128, 8, DM], F16)
        LNG = sb("lng", [128, DM], F32)
        LNB = sb("lnb", [128, DM], F32)
        GAMB = sb("gamb", [128, DM], F32)
        BETB = sb("betb", [128, DM], F32)
        OUTB = sb("outb", [128, DM], F32)
        INBS = sb("inb_s", [128, 24], F32)
        CONVW2 = sb("convw2_s", [128, 8, 3], F32)
        CONVB2 = sb("convb2_s", [128, 8], F32)
        ALOGB = sb("alogb_s", [128, 8, DSB], F32)
        BP = sb("bp_s", [128, 8, DSB], F32)
        CP = sb("cp_s", [128, 8, DSB], F32)
        DP = sb("dp_s", [128, 8], F32)
        KT = sb("kt", [128, 8, TAPS], F32)
        PIT = sb("pit", [128, 8, DSB], F32)
        IDENT = sb("ident2", [128, 128], F32)

        XA = sb("xa", [128, NT, DM], F32)
        XB = sb("xb", [128, NT, DM], F32)
        XC = sb("xc", [128, NT, DM], F32)
        XT = sb("xt", [128, 8, TOK2], F16)
        M_T = sb("m_t2", [128, NT], F32)
        Q_T = sb("q_t2", [128, NT], F32)
        RS_T = sb("rs_t2", [128, NT], F32)
        NM_T = sb("nm_t2", [128, NT], F32)

        URAW = [sb("uraw0", [128, 8, TOK2], F32), sb("uraw1", [128, 8, TOK2], F32)]
        SILV = [sb("silv0", [128, 8, TOK2], F16), sb("silv1", [128, 8, TOK2], F16)]
        SGG = [sb("sgg0", [128, 8, TOK2], F16), sb("sgg1", [128, 8, TOK2], F16)]
        UCX = sb("ucx", [128, 8, TAPS + TOK2], F16)
        UTAIL = sb("utail", [128, 8, 1], F32)
        CVT = sb("cvt", [128, 8, TOK2], F32)
        YB = sb("yb", [128, 8, TOK2], F32)
        GDB = sb("gdb", [128, 8, TOK2], F16)
        SCR = sb("scr", [128, 8 * TOK2], F32)

        PT = ps("pt2", [128, 512])
        MM0 = ps("mm02", [128, 512])

        for nm, dst in (("inw", INW), ("outw", OUTW)):
            total = int(np.prod(din[nm].shape[1:]))
            CHK = NT * DM
            src_f = flat(din[nm])
            dst_f = flat(dst)
            for off in range(0, total, CHK):
                w = min(CHK, total - off)
                scr = flat(XA)[:, 0:w]
                src = src_f[:, off:off + w]
                dst_h = dst_f[:, off:off + w]
                sc.dma(lambda e, src=src, scr=scr: e.dma_start(out=scr, in_=src))
                sc.stage("vector", lambda e, dst_h=dst_h, scr=scr:
                         e.tensor_copy(dst_h, scr))
        for nm, dst in (("lng_rep", LNG), ("lnb_rep", LNB), ("gamb_rep", GAMB),
                        ("betb_rep", BETB), ("outb_rep", OUTB), ("inb", INBS),
                        ("convw2", CONVW2), ("convb2", CONVB2), ("alogb", ALOGB),
                        ("bp", BP), ("cp", CP), ("dp", DP), ("ident", IDENT)):
            sc.dma(lambda e, nm=nm, dst=dst: e.dma_start(out=dst[:], in_=din[nm]))

        sc.stage("scalar",
                 lambda e: e.activation(out=ALOGB[:], in_=ALOGB[:], func=AF.Exp))
        sc.stage("scalar",
                 lambda e: e.activation(out=ALOGB[:], in_=ALOGB[:], func=AF.Exp,
                                        scale=-1.0))
        kfns = [lambda e: e.tensor_mul(PIT[:], BP[:], CP[:]),
                lambda e: e.tensor_reduce(out=KT[:, :, 0:1], in_=PIT[:],
                                          axis=AX.X, op=ALU.add)]
        for d in range(1, TAPS):
            kfns.append(lambda e: e.tensor_mul(PIT[:], PIT[:], ALOGB[:]))
            kfns.append(lambda e, d=d: e.tensor_reduce(
                out=KT[:, :, d:d + 1], in_=PIT[:], axis=AX.X, op=ALU.add))
        kfns.append(lambda e: e.memset(UCX[:, :, 0:TAPS], 0.0))
        sc.stage("vector", *kfns)

        def attnb_tail(kprev, cur_first):
            p = kprev % 2
            q = (kprev + 1) % 2
            fns = []
            for cb in range(8):
                fns.append(lambda e, cb=cb, p=p: e.tensor_scalar(
                    out=CVT[:, cb, :], in0=URAW[p][:, cb, :],
                    scalar1=CONVW2[:, cb, 1:2], scalar2=None, op0=ALU.mult))
                fns.append(lambda e, cb=cb, p=p, q=q: e.scalar_tensor_tensor(
                    out=CVT[:, cb, 1:TOK2], in0=URAW[p][:, cb, 0:TOK2 - 1],
                    scalar=CONVW2[:, cb, 0:1], in1=CVT[:, cb, 1:TOK2],
                    op0=ALU.mult, op1=ALU.add))
                if kprev > 0:
                    fns.append(lambda e, cb=cb: e.scalar_tensor_tensor(
                        out=CVT[:, cb, 0:1], in0=UTAIL[:, cb, :],
                        scalar=CONVW2[:, cb, 0:1], in1=CVT[:, cb, 0:1],
                        op0=ALU.mult, op1=ALU.add))
                fns.append(lambda e, cb=cb, p=p: e.scalar_tensor_tensor(
                    out=CVT[:, cb, 0:TOK2 - 1], in0=URAW[p][:, cb, 1:TOK2],
                    scalar=CONVW2[:, cb, 2:3], in1=CVT[:, cb, 0:TOK2 - 1],
                    op0=ALU.mult, op1=ALU.add))
                if cur_first is not None:
                    fns.append(lambda e, cb=cb, cur_first=cur_first:
                               e.scalar_tensor_tensor(
                                   out=CVT[:, cb, TOK2 - 1:TOK2],
                                   in0=cur_first[:, cb, 0:1],
                                   scalar=CONVW2[:, cb, 2:3],
                                   in1=CVT[:, cb, TOK2 - 1:TOK2],
                                   op0=ALU.mult, op1=ALU.add))
            sc.stage("vector", *fns)
            sc.stage("vector", *[
                (lambda e, cb=cb: e.tensor_scalar(
                    out=CVT[:, cb, :], in0=CVT[:, cb, :],
                    scalar1=CONVB2[:, cb:cb + 1], scalar2=None, op0=ALU.add))
                for cb in range(8)])
            sc.stage("scalar", lambda e: e.activation(
                out=SCR[:], in_=flat(CVT), func=AF.Exp, scale=-1.0))
            sc.stage(
                "vector",
                lambda e: e.tensor_scalar_add(SCR[:], SCR[:], 1.0),
                lambda e: e.reciprocal(out=SCR[:], in_=SCR[:]),
                lambda e: e.tensor_tensor(out=flat(CVT), in0=flat(CVT),
                                          in1=SCR[:], op=ALU.mult),
                lambda e: e.tensor_copy(UCX[:, :, TAPS:], CVT[:]),
            )
            tapfns = []
            for cb in range(8):
                tapfns.append(lambda e, cb=cb: e.tensor_scalar(
                    out=YB[:, cb, :], in0=CVT[:, cb, :],
                    scalar1=DP[:, cb:cb + 1], scalar2=None, op0=ALU.mult))
            for d in range(TAPS):
                for cb in range(8):
                    tapfns.append(lambda e, cb=cb, d=d: e.scalar_tensor_tensor(
                        out=YB[:, cb, :], in0=UCX[:, cb, TAPS - d:TAPS - d + TOK2],
                        scalar=KT[:, cb, d:d + 1], in1=YB[:, cb, :],
                        op0=ALU.mult, op1=ALU.add))
            sc.stage("vector", *tapfns, lag=8)
            sc.stage(
                "vector",
                lambda e: e.tensor_copy(UCX[:, :, 0:TAPS], UCX[:, :, TOK2:]),
                lambda e, p=p: e.tensor_tensor(
                    out=flat(CVT), in0=flat(YB), in1=flat(SGG[p]), op=ALU.mult),
                lambda e, p=p: e.tensor_tensor(
                    out=flat(GDB), in0=flat(CVT), in1=flat(SILV[p]), op=ALU.add),
            )
            for ob in range(8):
                sc.stage("tensor", *[
                    (lambda e, ob=ob, cb=cb: e.matmul(
                        MM0[:, 0:TOK2], OUTW[:, cb, ob * 128:(ob + 1) * 128],
                        GDB[:, cb, :], start=(cb == 0), stop=(cb == 7)))
                    for cb in range(8)])
                sc.stage("scalar", lambda e: e.activation(
                    out=flat(CVT)[:, 0:TOK2], in_=MM0[:, 0:TOK2], func=AF.Copy))
                sc.stage("tensor", *[
                    (lambda e, tb=tb: e.transpose(
                        PT[:, tb * 128:(tb + 1) * 128],
                        flat(CVT)[:, tb * 128:(tb + 1) * 128], IDENT[:]))
                    for tb in range(NT)])
                sc.stage("scalar", lambda e, ob=ob: e.activation(
                    out=XC[:, :, ob * 128:(ob + 1) * 128],
                    in_=PT[:, 0:NT * 128].rearrange("p (tb c) -> p tb c", tb=NT),
                    func=AF.Copy))
            sc.dma(lambda e, kprev=kprev: e.dma_start(out=pb2_t[kprev], in_=XC[:]),
                   store=True)

        for k in range(NCH2):
            par = k % 2
            sc.dma(lambda e, k=k: e.dma_start(out=XA[:], in_=t_in["parta"][k]))
            sc.dma(lambda e, k=k: e.dma_start(out=XB[:], in_=t_in["partb0"][k]))
            sc.stage("vector", lambda e: e.tensor_add(XA[:], XA[:], XB[:]))
            sc.dma(lambda e, k=k: e.dma_start(out=XB[:], in_=t_in["partb1"][k]))
            sc.stage("vector", lambda e: e.tensor_add(XA[:], XA[:], XB[:]))
            ln_full(sc, XA, XB, XC, M_T, Q_T, RS_T, NM_T, NT)
            ln_full(sc, XB, XC, XC, M_T, Q_T, RS_T, NM_T, NT)
            lg = LNG[:].unsqueeze(1).broadcast_to([128, NT, DM])
            lb = LNB[:].unsqueeze(1).broadcast_to([128, NT, DM])
            sc.stage(
                "vector",
                lambda e, lg=lg: e.tensor_tensor(out=XC[:], in0=XC[:], in1=lg,
                                                 op=ALU.mult),
                lambda e, lb=lb: e.tensor_tensor(out=XC[:], in0=XC[:], in1=lb,
                                                 op=ALU.add),
                lambda e: e.tensor_add(XA[:], XA[:], XC[:]),
            )
            sc.dma(lambda e, k=k: e.dma_start(out=x2_t[k], in_=XA[:]), store=True)
            ln_full(sc, XA, XB, XC, M_T, Q_T, RS_T, NM_T, NT)
            ln_full(sc, XB, XC, XC, M_T, Q_T, RS_T, NM_T, NT)
            gb = GAMB[:].unsqueeze(1).broadcast_to([128, NT, DM])
            bb = BETB[:].unsqueeze(1).broadcast_to([128, NT, DM])
            ob_ = OUTB[:].unsqueeze(1).broadcast_to([128, NT, DM])
            sc.stage(
                "vector",
                lambda e, gb=gb: e.tensor_tensor(out=XC[:], in0=XC[:], in1=gb,
                                                 op=ALU.mult),
                lambda e, bb=bb: e.tensor_tensor(out=XC[:], in0=XC[:], in1=bb,
                                                 op=ALU.add),
                lambda e: e.tensor_add(XB[:], XB[:], XA[:]),
                lambda e, ob_=ob_: e.tensor_tensor(out=XB[:], in0=XB[:], in1=ob_,
                                                   op=ALU.add),
            )
            sc.dma(lambda e, k=k: e.dma_start(out=pa2_t[k], in_=XB[:]), store=True)
            sc.stage("vector", lambda e, par=par: e.tensor_copy(
                UTAIL[:], URAW[par][:, :, TOK2 - 1:TOK2]))
            for db in range(8):
                sc.stage("tensor", *[
                    (lambda e, db=db, tb=tb: e.transpose(
                        PT[:, tb * 128:(tb + 1) * 128],
                        XC[:, tb, db * 128:(db + 1) * 128], IDENT[:]))
                    for tb in range(NT)])
                sc.stage("scalar", lambda e, db=db: e.activation(
                    out=XT[:, db, :], in_=PT[:, 0:NT * 128], func=AF.Copy))
            for grp, dst in ((0, None), (1, None), (2, None)):
                for obl in range(8):
                    po = grp * 8 + obl
                    sc.stage("tensor", *[
                        (lambda e, po=po, kb=kb: e.matmul(
                            MM0[:, 0:TOK2], INW[:, kb, po * 128:(po + 1) * 128],
                            XT[:, kb, :], start=(kb == 0), stop=(kb == 7)))
                        for kb in range(8)])
                    if grp == 0:
                        sc.stage("vector", lambda e, obl=obl, po=po, par=par:
                                 e.tensor_scalar(out=URAW[par][:, obl, :],
                                                 in0=MM0[:, 0:TOK2],
                                                 scalar1=INBS[:, po:po + 1],
                                                 scalar2=None, op0=ALU.add))
                    else:
                        sc.stage("vector", lambda e, obl=obl, po=po:
                                 e.tensor_scalar(out=CVT[:, obl, :],
                                                 in0=MM0[:, 0:TOK2],
                                                 scalar1=INBS[:, po:po + 1],
                                                 scalar2=None, op0=ALU.add))
                if grp == 1:
                    sc.stage("scalar", lambda e: e.activation(
                        out=SCR[:], in_=flat(CVT), func=AF.Exp, scale=-1.0))
                    sc.stage(
                        "vector",
                        lambda e: e.tensor_scalar_add(SCR[:], SCR[:], 1.0),
                        lambda e: e.reciprocal(out=SCR[:], in_=SCR[:]),
                        lambda e, par=par: e.tensor_tensor(
                            out=flat(SILV[par]), in0=flat(CVT), in1=SCR[:],
                            op=ALU.mult),
                    )
                elif grp == 2:
                    sc.stage("scalar", lambda e: e.activation(
                        out=SCR[:], in_=flat(CVT), func=AF.Exp, scale=-1.0))
                    sc.stage(
                        "vector",
                        lambda e: e.tensor_scalar_add(SCR[:], SCR[:], 1.0),
                        lambda e: e.reciprocal(out=SCR[:], in_=SCR[:]),
                        lambda e, par=par: e.tensor_copy(flat(SGG[par]), SCR[:]),
                    )
            if k > 0:
                attnb_tail(k - 1, URAW[par])
        attnb_tail(NCH2 - 1, None)

        sc.finish()
        run_streams(nc, sc)
    return nc


def build_l3():
    nc = bass.Bass("TRN2", target_bir_lowering=False)
    HL = L // 2
    din = {}
    def I(name, shape):
        din[name] = nc.dram_tensor(name, shape, F32, kind="ExternalInput").ap()
        return din[name]

    I("pa2", [HL, DM]); I("pb20", [HL, DM]); I("pb21", [HL, DM])
    I("lng_rep", [128, DM]); I("lnb_rep", [128, DM])
    dr_out = nc.dram_tensor("outh", [HL, DM], F32, kind="ExternalOutput").ap()

    NT = 4
    TOKL = 512
    t_in = {nm: din[nm].rearrange("(k tb p) d -> k p tb d", p=128, tb=NT)
            for nm in ("pa2", "pb20", "pb21")}
    out_t = dr_out.rearrange("(k tb p) d -> k p tb d", p=128, tb=NT)

    with contextlib.ExitStack() as ctx:
        sb = lambda nm, shp, dt: ctx.enter_context(nc.sbuf_tensor(nm, shp, dt))
        sems = {nm: ctx.enter_context(nc.semaphore(nm))
                for nm in ("ld", "st", *ENGS)}
        sc = Sched(sems)
        CONSTS = sb("consts3", [128, 2], F32)
        setup_consts(nc, sc, CONSTS)

        LNG = sb("lng3", [128, DM], F32)
        LNB = sb("lnb3", [128, DM], F32)
        XA = sb("xa3", [128, NT, DM], F32)
        XB = sb("xb3", [128, NT, DM], F32)
        XC = sb("xc3", [128, NT, DM], F32)
        M_T = sb("m_t3", [128, NT], F32)
        Q_T = sb("q_t3", [128, NT], F32)
        RS_T = sb("rs_t3", [128, NT], F32)
        NM_T = sb("nm_t3", [128, NT], F32)

        sc.dma(lambda e: e.dma_start(out=LNG[:], in_=din["lng_rep"]))
        sc.dma(lambda e: e.dma_start(out=LNB[:], in_=din["lnb_rep"]))

        for k in range(HL // TOKL):
            sc.dma(lambda e, k=k: e.dma_start(out=XA[:], in_=t_in["pa2"][k]))
            sc.dma(lambda e, k=k: e.dma_start(out=XB[:], in_=t_in["pb20"][k]))
            sc.stage("vector", lambda e: e.tensor_add(XA[:], XA[:], XB[:]))
            sc.dma(lambda e, k=k: e.dma_start(out=XB[:], in_=t_in["pb21"][k]))
            sc.stage("vector", lambda e: e.tensor_add(XA[:], XA[:], XB[:]))
            ln_full(sc, XA, XB, XC, M_T, Q_T, RS_T, NM_T, NT)
            ln_full(sc, XB, XC, XC, M_T, Q_T, RS_T, NM_T, NT)
            lg = LNG[:].unsqueeze(1).broadcast_to([128, NT, DM])
            lb = LNB[:].unsqueeze(1).broadcast_to([128, NT, DM])
            sc.stage(
                "vector",
                lambda e, lg=lg: e.tensor_tensor(out=XC[:], in0=XC[:], in1=lg,
                                                 op=ALU.mult),
                lambda e, lb=lb: e.tensor_tensor(out=XC[:], in0=XC[:], in1=lb,
                                                 op=ALU.add),
                lambda e: e.tensor_add(XA[:], XA[:], XC[:]),
            )
            sc.dma(lambda e, k=k: e.dma_start(out=out_t[k], in_=XA[:]), store=True)

        sc.finish()
        run_streams(nc, sc)
    return nc


def _tile_w(w, kblocks):
    K, M = w.shape
    assert K == kblocks * 128
    return np.ascontiguousarray(w.reshape(kblocks, 128, M).transpose(1, 0, 2))


def _tile_b(v, nblocks):
    return np.ascontiguousarray(v.reshape(nblocks, 128).T)


def _rep(v):
    return np.ascontiguousarray(np.broadcast_to(np.asarray(v, np.float32),
                                                (128, DM)))


def stage_l1_maps(inp):
    f = lambda x: np.ascontiguousarray(np.asarray(x, np.float32))
    ident = np.eye(128, dtype=np.float32)
    x = f(inp["L0_attn_a_x"])
    W_up = f(inp["L0_attn_a_W_up"]); b_up = f(inp["L0_attn_a_b_up"])
    conv_w = f(inp["L0_attn_a_conv_w"]); conv_b = f(inp["L0_attn_a_conv_b"])
    W_delta = f(inp["L0_attn_a_W_delta"]); b_delta = f(inp["L0_attn_a_b_delta"])
    W_B = f(inp["L0_attn_a_W_B"]); W_C = f(inp["L0_attn_a_W_C"])
    A_log = f(inp["L0_attn_a_A_log"]); Dv = f(inp["L0_attn_a_D"])
    W_down = f(inp["L0_attn_a_W_down"]); b_down = f(inp["L0_attn_a_b_down"])
    gam = f(inp["L0_attn_a_ln_gamma"]); bet = f(inp["L0_attn_a_ln_beta"])

    maps1 = []
    for c in range(8):
        b, j = divmod(c, 2)
        my = slice(j * CHA, (j + 1) * CHA)
        perm = np.concatenate([np.arange(j * CHA, (j + 1) * CHA),
                               np.arange((1 - j) * CHA, (1 - j) * CHA + CHA)])
        maps1.append(dict(
            x0=np.ascontiguousarray(x[b]),
            gam_rep=_rep(gam), bet_rep=_rep(bet),
            bdcol=_tile_b(b_down, 8) if j == 0 else np.zeros((128, 8), np.float32),
            wup_l=_tile_w(W_up[:, 0:DI][:, perm], 8),
            wup_r=_tile_w(W_up[:, DI + j * CHA: DI + (j + 1) * CHA], 8),
            bup_l=_tile_b(b_up[0:DI][perm], 4),
            bup_r=_tile_b(b_up[DI + j * CHA: DI + (j + 1) * CHA], 2),
            convw=np.ascontiguousarray(
                conv_w[perm, 0, :].reshape(4, 128, 3).transpose(1, 0, 2)),
            convb=_tile_b(conv_b[perm], 4),
            wdelta=_tile_w(W_delta[perm][:, my], 4),
            bdelta=_tile_b(b_delta[my], 2),
            wb=_tile_w(W_B[perm], 4),
            wc=_tile_w(W_C[perm], 4),
            alog=np.ascontiguousarray(
                A_log[my].reshape(2, 128, NS).transpose(1, 0, 2)),
            dvec=_tile_b(Dv[my], 2),
            wdown=_tile_w(W_down[my], 2),
            ident=ident,
        ))
    return maps1


def stage_l2_maps(inp, r1):
    f = lambda x: np.ascontiguousarray(np.asarray(x, np.float32))
    ident = np.eye(128, dtype=np.float32)
    in_w = f(inp["L0_attn_b_in_w"]); in_b = f(inp["L0_attn_b_in_b"])
    conv_w2 = f(inp["L0_attn_b_conv_w"]); conv_b2 = f(inp["L0_attn_b_conv_b"])
    A_logb = f(inp["L0_attn_b_A_log"]); Bpv = f(inp["L0_attn_b_B_p"])
    Cpv = f(inp["L0_attn_b_C_p"]); Dpv = f(inp["L0_attn_b_D_p"])
    out_w = f(inp["L0_attn_b_out_w"]); out_b = f(inp["L0_attn_b_out_b"])
    gamb = f(inp["L0_attn_b_norm_gamma"]); betb = f(inp["L0_attn_b_norm_beta"])
    lng = f(inp["L0_ffn_ln_g"]); lnb = f(inp["L0_ffn_ln_b"])

    maps2 = []
    for c in range(8):
        b, j = divmod(c, 2)
        myb = slice(j * CHB, (j + 1) * CHB)
        inw_cat = np.concatenate([in_w[:, j * CHB:(j + 1) * CHB],
                                  in_w[:, INB + j * CHB: INB + (j + 1) * CHB],
                                  in_w[:, 2 * INB + j * CHB: 2 * INB + (j + 1) * CHB]],
                                 axis=1)
        inb_cat = np.concatenate([in_b[j * CHB:(j + 1) * CHB],
                                  in_b[INB + j * CHB: INB + (j + 1) * CHB],
                                  in_b[2 * INB + j * CHB: 2 * INB + (j + 1) * CHB]])
        maps2.append(dict(
            parta=r1[2 * b]["parta"],
            partb0=r1[2 * b]["partb"],
            partb1=r1[2 * b + 1]["partb"],
            lng_rep=_rep(lng), lnb_rep=_rep(lnb),
            gamb_rep=_rep(gamb), betb_rep=_rep(betb),
            inw=_tile_w(inw_cat, 8),
            inb=_tile_b(inb_cat, 24),
            convw2=np.ascontiguousarray(
                conv_w2[myb, 0, :].reshape(8, 128, 3).transpose(1, 0, 2)),
            convb2=_tile_b(conv_b2[myb], 8),
            alogb=np.ascontiguousarray(
                A_logb[myb].reshape(8, 128, DSB).transpose(1, 0, 2)),
            bp=np.ascontiguousarray(
                Bpv[myb].reshape(8, 128, DSB).transpose(1, 0, 2)),
            cp=np.ascontiguousarray(
                Cpv[myb].reshape(8, 128, DSB).transpose(1, 0, 2)),
            dp=_tile_b(Dpv[myb], 8),
            outw=_tile_w(out_w[myb], 8),
            outb_rep=_rep(out_b) if j == 0 else np.zeros((128, DM), np.float32),
            ident=ident,
        ))
    return maps2


def stage_l3_maps(inp, r2):
    f = lambda x: np.ascontiguousarray(np.asarray(x, np.float32))
    lng = f(inp["L0_ffn_ln_g"]); lnb = f(inp["L0_ffn_ln_b"])
    HL = L // 2
    maps3 = []
    for c in range(8):
        b, j = divmod(c, 2)
        rows = slice(j * HL, (j + 1) * HL)
        maps3.append(dict(
            pa2=np.ascontiguousarray(r2[2 * b]["parta2"][rows]),
            pb20=np.ascontiguousarray(r2[2 * b]["partb2"][rows]),
            pb21=np.ascontiguousarray(r2[2 * b + 1]["partb2"][rows]),
            lng_rep=_rep(lng), lnb_rep=_rep(lnb),
        ))
    return maps3


def kernel(**inp):
    maps1 = stage_l1_maps(inp)
    nc1 = build_l1()
    r1 = run_bass_kernel_spmd(nc1, maps1, core_ids=list(range(8))).results
    maps2 = stage_l2_maps(inp, r1)
    nc2 = build_l2()
    r2 = run_bass_kernel_spmd(nc2, maps2, core_ids=list(range(8))).results
    maps3 = stage_l3_maps(inp, r2)
    nc3 = build_l3()
    r3 = run_bass_kernel_spmd(nc3, maps3, core_ids=list(range(8))).results

    out = np.empty((B, L, DM), np.float32)
    hl = L // 2
    for c in range(8):
        b, j = divmod(c, 2)
        out[b, j * hl:(j + 1) * hl] = r3[c]["outh"]
    return out
